# version 18
# baseline (speedup 1.0000x reference)
import contextlib
import numpy as np
import ml_dtypes
import concourse.bass as bass
import concourse.mybir as mybir
from concourse.bass_utils import run_bass_kernel_spmd

BF16 = ml_dtypes.bfloat16
F32 = mybir.dt.float32
BF = mybir.dt.bfloat16
I32 = mybir.dt.int32
AF = mybir.ActivationFunctionType
ALU = mybir.AluOpType

B, S, D = 16, 2048, 1024
H, DK = 16, 64
HID = 3072
NC = 8
SEQ_PER_CORE = B // NC
TOK = SEQ_PER_CORE * S
NT_SEQ = S // 128
NG = 4
FFN_SPLIT = [0, 4, 20, 44, 72]
NHC = HID // 128
EPS_LN = 1e-5
EPS_RMS = 1.1920929e-07
SOFTCAP = 50.0
DEQ = 1.0 / 127.0


class Ctr:
    def __init__(self, h, step=1):
        self.h = h
        self.n = 0
        self.step = step

    def inc(self, instr, k=None):
        k = self.step if k is None else k
        instr.then_inc(self.h, k)
        self.n += k
        return self.n


def build_nc(stop=None, dbg=()):
    nc = bass.Bass("TRN2", target_bir_lowering=False)

    x_h = nc.dram_tensor("x", [TOK, D], F32, kind="ExternalInput")
    wq_h = nc.dram_tensor("wq", [D, D], F32, kind="ExternalInput")
    wk_h = nc.dram_tensor("wk", [D, D], F32, kind="ExternalInput")
    wv_h = nc.dram_tensor("wv", [D, D], F32, kind="ExternalInput")
    wo_h = nc.dram_tensor("wo", [D, D], F32, kind="ExternalInput")
    wg_h = nc.dram_tensor("wg", [HID, D], I32, kind="ExternalInput")
    wu_h = nc.dram_tensor("wu", [HID, D], I32, kind="ExternalInput")
    wd_h = nc.dram_tensor("wd", [D, HID], I32, kind="ExternalInput")
    cut_h = nc.dram_tensor("cut", [128, 128], BF, kind="ExternalInput")
    csel_h = nc.dram_tensor("csel", [128, 128], BF, kind="ExternalInput")
    cselb_h = nc.dram_tensor("cselb", [16, 1024], BF, kind="ExternalInput")
    cones_h = nc.dram_tensor("cones", [1, 64], BF, kind="ExternalInput")
    cconst_h = nc.dram_tensor("cconst", [128, 4], F32, kind="ExternalInput")
    cid_h = nc.dram_tensor("cid", [128, 128], BF, kind="ExternalInput")
    out_h = nc.dram_tensor("out", [TOK, D], F32, kind="ExternalOutput")

    wg_s = nc.dram_tensor("wg_s", [HID, D], BF, kind="Internal")
    wu_s = nc.dram_tensor("wu_s", [HID, D], BF, kind="Internal")
    wd_s = nc.dram_tensor("wd_s", [D, HID], BF, kind="Internal")
    wgt_s = nc.dram_tensor("wgt_s", [NHC, 128, 1024], BF, kind="Internal")
    wut_s = nc.dram_tensor("wut_s", [NHC, 128, 1024], BF, kind="Internal")
    wdt2_s = nc.dram_tensor("wdt2_s", [2, 128, NHC, 512], BF, kind="Internal")
    h_s = nc.dram_tensor("h_s", [TOK, D], BF, kind="Internal")
    h2_s = nc.dram_tensor("h2_s", [TOK, D], BF, kind="Internal")
    x2_s = nc.dram_tensor("x2_s", [TOK, D], BF, kind="Internal")

    es = contextlib.ExitStack()
    sbt = lambda name, shape, dt: es.enter_context(nc.sbuf_tensor(name, shape, dt))
    pst = lambda name, shape, dt: es.enter_context(nc.psum_tensor(name, shape, dt))
    mks = lambda name: es.enter_context(nc.semaphore(name))

    wqT = sbt("wqT", [128, 8192], BF)
    wkT = sbt("wkT", [128, 8192], BF)
    wvT = sbt("wvT", [128, 8192], BF)
    woT = sbt("woT", [128, 8192], BF)
    kTact = sbt("kTact", [128, 16384], BF)
    vbuf = sbt("vbuf", [128, 16640], BF)
    xb = sbt("xb", [128, 8192], BF)
    hTq = sbt("hTq", [128, 4096], BF)
    qyh = sbt("qyh", [128, 8192], BF)
    htm = sbt("htm", [128, 2048], BF)
    x2b = sbt("x2b", [128, 1024], BF)
    tanhsil = sbt("tanhsil", [128, 2048], F32)
    pTb = sbt("pTb", [128, 2048], BF)
    sqjunk = sbt("sqjunk", [128, 1024], BF)
    outb = sbt("outb", [128, 512], F32)
    wgub = sbt("wgub", [128, 2048], BF)
    wuub = sbt("wuub", [128, 2048], BF)
    sln = sbt("sln", [128, 32], F32)
    r48f = sbt("r48f", [48, 512], F32)
    rqk_b = sbt("rqk_b", [48, 512], BF)
    recb = rqk_b[0:1, :]
    rb = sbt("rb", [64, 512], BF)
    cut = sbt("cutb", [128, 128], BF)
    csel = sbt("cself", [128, 128], BF)
    cselb = sbt("cselbb", [48, 1024], BF)
    cones = sbt("conesb", [1, 64], BF)
    cconst = sbt("cconstb", [128, 4], F32)
    cid = sbt("cidb", [128, 128], BF)

    s01 = pst("s01", [128, 1024], F32)
    s23 = pst("s23", [128, 1024], F32)
    f4 = pst("f4", [128, 512], F32)
    f5 = pst("f5", [128, 512], F32)
    f6 = pst("f6", [128, 512], F32)
    f7 = pst("f7", [128, 512], F32)
    pqv = [s01[:, 0:512], s01[:, 512:1024]]
    selv = s23[:, 0:512]
    bcv = [f4, f5]
    pov = [f6, f7]
    pdv = [s23[:, 0:512], s23[:, 512:1024], f4[:, :], f5[:, :]]
    spair = [s01, s23]
    pyv = [f4, f5]
    ybcv = [f6, f7]

    sem_x = [Ctr(mks("sem_x0"), 16), Ctr(mks("sem_x1"), 16)]
    sem_wcf = Ctr(mks("sem_wcf"), 16)
    sem_wl = [Ctr(mks("sem_wl0"), 16), Ctr(mks("sem_wl1"), 16)]
    sem_cst = Ctr(mks("sem_cst"), 16)
    sem_fpp = [Ctr(mks("sem_fpp0"), 16), Ctr(mks("sem_fpp1"), 16)]
    sem_hw = Ctr(mks("sem_hw"), 16)
    sem_hr = Ctr(mks("sem_hr"), 16)
    sem_h2w = Ctr(mks("sem_h2w"), 16)
    sem_h2r = [Ctr(mks("sem_h2r0"), 16), Ctr(mks("sem_h2r1"), 16)]
    sem_x2w = Ctr(mks("sem_x2w"), 16)
    sem_x2r = Ctr(mks("sem_x2r"), 16)
    sem_fw = [Ctr(mks("sem_fw0"), 16), Ctr(mks("sem_fw1"), 16)]
    sem_fwd = Ctr(mks("sem_fwd"), 16)
    sem_out = Ctr(mks("sem_out"), 16)
    c_pe = Ctr(mks("c_pe"))
    c_gp = Ctr(mks("c_gp"))
    c_act = Ctr(mks("c_act"))
    c_dve = Ctr(mks("c_dve"))

    pe, act, dve, gp, sp = nc.tensor, nc.scalar, nc.vector, nc.gpsimd, nc.sync

    x_done = {}

    def load_x(ti):
        sx = sem_x[(ti // 4) % 2]
        i = gp.dma_start(xb[:, (ti % 8) * 1024:(ti % 8 + 1) * 1024],
                         x_h[ti * 128:(ti + 1) * 128, :])
        sx.inc(i)
        x_done[ti] = True

    for dst, src in ((cut[:, :], cut_h), (csel[:, :], csel_h),
                     (cselb[0:16, :], cselb_h), (cselb[32:48, :], cselb_h),
                     (cones[:, :], cones_h), (cconst[:, :], cconst_h),
                     (cid[:, :], cid_h)):
        sem_cst.inc(sp.dma_start(dst, src[:, :]))
    consts_done = sem_cst.n

    for ti in range(8):
        load_x(ti)
    for nm, (src, dst) in dict(
            g=(wg_h, wg_s), u=(wu_h, wu_s), d=(wd_h, wd_s)).items():
        R = dst.shape[0] // 16
        for c in range(16):
            sem_wcf.inc(gp.dma_start(dst[c * R:(c + 1) * R, :],
                                     src[c * R:(c + 1) * R, :]))

    i = dve.memset(r48f[0:32, :], 1.0)
    c_dve.inc(i)
    r48f_init = c_dve.n
    act.wait_ge(sem_cst.h, consts_done)
    act.wait_ge(c_dve.h, r48f_init)
    dve.wait_ge(sem_cst.h, consts_done)
    gp.wait_ge(sem_cst.h, consts_done)
    pe.wait_ge(sem_cst.h, consts_done)

    cp_ev = {}
    tp_ev = {}
    ktact_mm_last = {}
    def emit_wT(wi, src, dst):
        half = (wi % 2) * 8192
        if wi >= 2:
            sp.wait_ge(c_pe.h, ktact_mm_last[wi - 2])
        for ro in range(8):
            t = wi * 8 + ro
            sl = tanhsil[:, (t % 2) * 1024:(t % 2 + 1) * 1024]
            if t >= 2:
                sp.wait_ge(c_dve.h, cp_ev[t - 2])
            sem_wl[t % 2].inc(sp.dma_start(sl, src[ro * 128:(ro + 1) * 128, :]))
            dve.wait_ge(sem_wl[t % 2].h, sem_wl[t % 2].n)
            i = dve.tensor_copy(kTact[:, half + ro * 1024: half + (ro + 1) * 1024], sl)
            c_dve.inc(i)
            cp_ev[t] = c_dve.n
        pe.wait_ge(c_dve.h, cp_ev[wi * 8 + 7])
        for di in range(8):
            tb = wi * 8 + di
            b0, b1 = (f6, f7) if tb % 2 == 0 else (f4, f5)
            if tb >= 2:
                pe.wait_ge(c_dve.h, tp_ev[tb - 2])
            for qo in range(8):
                bank = b0 if qo < 4 else b1
                i = pe.matmul(bank[:, (qo % 4) * 128:(qo % 4 + 1) * 128],
                              kTact[:, half + qo * 1024 + di * 128:
                                    half + qo * 1024 + (di + 1) * 128],
                              cid[:, :], start=True, stop=True)
            c_pe.inc(i)
            dve.wait_ge(c_pe.h, c_pe.n)
            dve.tensor_copy(dst[:, di * 1024: di * 1024 + 512], b0[:, :])
            i = dve.tensor_copy(dst[:, di * 1024 + 512:(di + 1) * 1024], b1[:, :])
            c_dve.inc(i)
            tp_ev[tb] = c_dve.n
        ktact_mm_last[wi] = c_pe.n
    for wi, (src, dst) in enumerate(
            [(wq_h, wqT), (wk_h, wkT), (wv_h, wvT), (wo_h, woT)]):
        emit_wT(wi, src, dst)
    attn_w_done = c_dve.n

    fpp_done = {}
    ffn_steps = []
    def mk_ffn_step(src, dst2, h0, di, hc, pidx, is_wd):
        def go():
            buf = wgub if pidx % 2 == 0 else wuub
            par = pidx % 2
            if pidx == 0:
                sp.wait_ge(sem_wcf.h, sem_wcf.n)
            if pidx >= 2:
                sp.wait_ge(sem_fpp[par].h, fpp_done[pidx - 2])
            if is_wd:
                sem_fpp[par].inc(sp.dma_start(
                    buf[:, 0:1024], src[0:1024, hc * 128:(hc + 1) * 128],
                    transpose=True))
                sp.wait_ge(sem_fpp[par].h, sem_fpp[par].n)
                sem_fpp[par].inc(sp.dma_start(
                    dst2[0, :, hc, :], buf[:, 0:512]))
                sem_fpp[par].inc(sp.dma_start(
                    dst2[1, :, hc, :], buf[:, 512:1024]))
            else:
                sem_fpp[par].inc(sp.dma_start(
                    buf[:, 0:1024], src[h0:h0 + 1024, di * 128:(di + 1) * 128],
                    transpose=True))
                sp.wait_ge(sem_fpp[par].h, sem_fpp[par].n)
                sem_fpp[par].inc(sp.dma_start(
                    dst2[h0 // 128:h0 // 128 + 8, :, di * 128:(di + 1) * 128]
                    .rearrange("c p m -> p c m"),
                    buf[:, 0:1024].rearrange("p (c m) -> p c m", m=128)))
            fpp_done[pidx] = sem_fpp[par].n
        return go
    pidx = 0
    for src, dst2 in ((wg_s, wgt_s), (wu_s, wut_s)):
        for h0 in range(0, HID, 1024):
            for di in range(8):
                ffn_steps.append(mk_ffn_step(src, dst2, h0, di, 0, pidx, False))
                pidx += 1
    for hc in range(NHC):
        ffn_steps.append(mk_ffn_step(wd_s, wdt2_s, 0, 0, hc, pidx, True))
        pidx += 1

    if stop == ("PREP", 0, 0):
        gp.wait_ge(sem_x[0].h, sem_x[0].n)
        gp.wait_ge(sem_x[1].h, sem_x[1].n)
        es.close()
        return nc

    def emit_ln(srcap, dstap, xwait=None):
        if xwait is not None:
            dve.wait_ge(xwait[0], xwait[1])
        dve.tensor_scalar(sqjunk[:, :], srcap, 1.0, 0.0, ALU.mult, ALU.add,
                          accum_out=sln[:, 0:1])
        dve.drain()
        dve.scalar_tensor_tensor(sqjunk[:, :], srcap, 1.0, srcap, ALU.mult,
                                 ALU.mult, accum_out=sln[:, 1:2])
        dve.drain()
        dve.tensor_scalar(sln[:, 2:3], sln[:, 0:1], 1.0 / D, None, ALU.mult)
        dve.tensor_scalar(sln[:, 3:4], sln[:, 1:2], 1.0 / D, None, ALU.mult)
        dve.drain()
        dve.scalar_tensor_tensor(sln[:, 4:5], sln[:, 2:3], 1.0, sln[:, 2:3],
                                 ALU.mult, ALU.mult)
        dve.drain()
        i = dve.tensor_tensor(sln[:, 5:6], sln[:, 3:4], sln[:, 4:5], ALU.subtract)
        c_dve.inc(i)
        act.wait_ge(c_dve.h, c_dve.n)
        i = act.activation(sln[:, 6:7], sln[:, 5:6], AF.Sqrt,
                           bias=cconst[:, 0:1], scale=1.0)
        c_act.inc(i)
        dve.wait_ge(c_act.h, c_act.n)
        dve.reciprocal(sln[:, 7:8], sln[:, 6:7])
        dve.drain()
        dve.tensor_tensor(sln[:, 8:9], sln[:, 2:3], sln[:, 7:8], ALU.mult)
        dve.drain()
        i = dve.tensor_scalar(dstap, srcap, sln[:, 7:8], sln[:, 8:9],
                              ALU.mult, ALU.subtract)
        return c_dve.inc(i)

    ev = {}
    carry = {
        "tanh_s01": None, "tanh_s23": None,
        "pv_slot": {0: None, 1: None},
        "smm_last": 0,
        "pv_last_final": 0, "ynorm15": 0,
        "a1_mm_last": 0, "a1_copy_last": 0,
        "a2_apply_last": 0,
        "sel_last": 0,
        "ln2_last": 0,
        "x2_res_last": 0,
        "po_free": {0: 0, 1: 0},
        "b_act_end": 0, "b_dve_end": 0,
        "omm_last": 0, "a2f6_free": 0,
    }

    def hch(di):
        if di < 4:
            return htm[:, di * 512:(di + 1) * 512]
        if di < 6:
            return sqjunk[:, (di - 4) * 512:(di - 5 + 2) * 512]
        return x2b[:, (di - 6) * 512:(di - 5) * 512]

    def emit_ln1_part1(s_i, g):
        tbase = s_i * NT_SEQ + 4 * g
        dve.wait_ge(sem_h2w.h, sem_h2w.n)
        dve.wait_ge(sem_x2w.h, sem_x2w.n)
        dve.wait_ge(sem_hw.h, sem_hw.n)
        dve.wait_ge(c_pe.h, carry["sel_last"])
        dve.wait_ge(c_pe.h, carry["a1_mm_last"])
        sxp = sem_x[(s_i * 4 + g) % 2]
        dve.wait_ge(sxp.h, sxp.n)
        lane = lambda l: sln[:, :].rearrange("p (t l) -> p t l", l=8)[:, :, l:l + 1]
        for tl in range(4):
            ti = tbase + tl
            xa = xb[:, (ti % 8) * 1024:(ti % 8 + 1) * 1024]
            dve.tensor_scalar(sqjunk[:, :], xa, 1.0, 0.0, ALU.mult, ALU.add,
                              accum_out=sln[:, 8 * tl: 8 * tl + 1])
            dve.drain()
            dve.scalar_tensor_tensor(sqjunk[:, :], xa, 1.0, xa, ALU.mult,
                                     ALU.mult, accum_out=sln[:, 8 * tl + 1: 8 * tl + 2])
            dve.drain()
        dve.tensor_scalar(lane(2), lane(0), 1.0 / D, None, ALU.mult)
        dve.tensor_scalar(lane(3), lane(1), 1.0 / D, None, ALU.mult)
        dve.drain()
        dve.scalar_tensor_tensor(lane(4), lane(2), 1.0, lane(2), ALU.mult, ALU.mult)
        dve.drain()
        i = dve.tensor_tensor(lane(5), lane(3), lane(4), ALU.subtract)
        c_dve.inc(i)
        carry["ln1_stats_done"] = c_dve.n

    def emit_ln1_part2(s_i, g, inside_b=False):
        tbase = s_i * NT_SEQ + 4 * g
        lane = lambda l: sln[:, :].rearrange("p (t l) -> p t l", l=8)[:, :, l:l + 1]
        act.wait_ge(c_dve.h, carry["ln1_stats_done"])
        if inside_b:
            act.preload_act_table(AF.Sqrt)
        i = act.activation(lane(6), lane(5), AF.Sqrt, bias=cconst[:, 0:1], scale=1.0)
        c_act.inc(i)
        if inside_b:
            act.preload_act_table(AF.Exp)
        dve.wait_ge(c_act.h, c_act.n)
        dve.reciprocal(lane(6), lane(6))
        dve.drain()
        dve.tensor_tensor(lane(7), lane(2), lane(6), ALU.mult)
        dve.drain()
        for tl in range(4):
            ti = tbase + tl
            xa = xb[:, (ti % 8) * 1024:(ti % 8 + 1) * 1024]
            if tl >= 2:
                dve.wait_ge(sem_hw.h, sem_hw.n)
            i = dve.tensor_scalar(htm[:, (tl % 2) * 1024:(tl % 2 + 1) * 1024],
                                  xa, sln[:, 8 * tl + 6: 8 * tl + 7],
                                  sln[:, 8 * tl + 7: 8 * tl + 8],
                                  ALU.mult, ALU.subtract)
            n = c_dve.inc(i)
            gp.wait_ge(c_dve.h, n)
            i = gp.dma_start(h_s[ti * 128:(ti + 1) * 128, :],
                             htm[:, (tl % 2) * 1024:(tl % 2 + 1) * 1024])
            sem_hw.inc(i)

    def emit_hT_load(s_i, g):
        tbase = s_i * NT_SEQ + 4 * g
        sp.wait_ge(sem_hw.h, sem_hw.n)
        sp.wait_ge(sem_h2w.h, sem_h2w.n)
        sp.wait_ge(sem_x2w.h, sem_x2w.n)
        sp.wait_ge(c_pe.h, carry["sel_last"])
        sp.wait_ge(c_pe.h, carry["a1_mm_last"])
        sp.wait_ge(c_dve.h, c_dve.n)
        for di in range(8):
            i = sp.dma_start(hch(di),
                             h_s[tbase * 128:tbase * 128 + 512,
                                 di * 128:(di + 1) * 128],
                             transpose=True)
            sem_hr.inc(i)

    def make_a1_units(s_i, g, standalone=False):
        tbase = s_i * NT_SEQ + 4 * g
        hT_done = sem_hr.n
        state = {"prev_copy": None, "prev2_copy": None}

        def unit(j):
            def go():
                if standalone:
                    bank = s01[:, (j % 2) * 512:(j % 2 + 1) * 512]
                else:
                    bank = f7[:, :]
                if j == 0:
                    pe.wait_ge(sem_hr.h, hT_done)
                    pe.wait_ge(c_dve.h, carry["a2_apply_last"])
                    if s_i == 0 and g == 0:
                        pe.wait_ge(c_dve.h, attn_w_done)
                elif standalone:
                    if state["prev2_copy"] is not None:
                        pe.wait_ge(c_dve.h, state["prev2_copy"])
                else:
                    pe.wait_ge(c_dve.h, state["prev_copy"])
                if j < 16:
                    which = "q" if j < 8 else "k"
                    m = j % 8
                    wT = wqT if which == "q" else wkT
                    for di in range(8):
                        i = pe.matmul(bank,
                                      wT[:, di * 1024 + m * 128: di * 1024 + (m + 1) * 128],
                                      hch(di),
                                      start=(di == 0), stop=(di == 7))
                    c_pe.inc(i)
                    dve.wait_ge(c_pe.h, c_pe.n)
                    if which == "q":
                        if j == 0:
                            dve.wait_ge(c_pe.h, carry["omm_last"])
                        dst = qyh[:, 4096 + m * 512: 4096 + (m + 1) * 512]
                    else:
                        dst = kTact[:, m * 2048 + g * 512: m * 2048 + (g + 1) * 512]
                    i = dve.tensor_copy(dst, bank)
                else:
                    tl, half = (j - 16) // 2, (j - 16) % 2
                    for di in range(8):
                        i = pe.matmul(bank,
                                      hch(di)[:, tl * 128:(tl + 1) * 128],
                                      wvT[:, di * 1024 + half * 512: di * 1024 + (half + 1) * 512],
                                      start=(di == 0), stop=(di == 7))
                    c_pe.inc(i)
                    dve.wait_ge(c_pe.h, c_pe.n)
                    tseq = 4 * g + tl
                    vt = vbuf[:, tseq * 1040:(tseq + 1) * 1040].rearrange(
                        "p (hh dd) -> p hh dd", dd=65)
                    i = dve.tensor_copy(vt[:, half * 8:(half + 1) * 8, 0:64],
                                        bank.rearrange("p (hh dd) -> p hh dd", dd=64))
                c_dve.inc(i)
                state["prev2_copy"] = state["prev_copy"]
                state["prev_copy"] = c_dve.n
                carry["a1_mm_last"] = c_pe.n
                carry["a1_copy_last"] = c_dve.n
            return go
        return [unit(j) for j in range(24)]

    def emit_A2(s_i, g):
        act.preload_act_table(AF.Sqrt)
        act.wait_ge(c_pe.h, carry["a1_mm_last"])
        act.wait_ge(c_dve.h, carry["a1_copy_last"])
        for p in range(16):
            which = "q" if p % 2 == 0 else "k"
            m = p // 2
            if which == "q":
                src = qyh[:, 4096 + m * 512: 4096 + (m + 1) * 512]
            else:
                src = kTact[:, m * 2048 + g * 512: m * 2048 + (g + 1) * 512]
            if p >= 2:
                act.wait_ge(c_pe.h, ev[("sel", which, m - 1)])
            i = act.activation(sqjunk[:, (p % 2) * 512:(p % 2 + 1) * 512],
                               src, AF.Square)
            c_act.inc(i)
            ev[("sq", which, m)] = c_act.n
            msout = s23[0:16, 0:512] if which == "q" else s23[32:48, 512:1024]
            pe.wait_ge(c_act.h, ev[("sq", which, m)])
            if p <= 1:
                tkey = carry["tanh_s23"]
                if tkey is not None:
                    pe.wait_ge(c_act.h, tkey)
            i = pe.matmul(msout,
                          csel[:, m * 16:(m + 1) * 16],
                          sqjunk[:, (p % 2) * 512:(p % 2 + 1) * 512],
                          start=(p <= 1), stop=(p >= 14))
            c_pe.inc(i)
            ev[("sel", which, m)] = c_pe.n
        carry["sel_last"] = c_pe.n
        act.wait_ge(c_pe.h, ev[("sel", "k", 7)])
        act.activation(r48f[0:16, :], s23[0:16, 0:512], AF.Sqrt,
                       bias=cconst[0:16, 1:2], scale=1.0 / 64.0)
        act.drain()
        i = act.activation(r48f[32:48, :], s23[32:48, 512:1024], AF.Sqrt,
                           bias=cconst[0:16, 1:2], scale=1.0 / 64.0)
        c_act.inc(i)
        ev["rmssqrt"] = c_act.n
        dve.wait_ge(c_act.h, c_act.n)
        with nc.allow_low_precision(reason="rms rstd recip bf16; tolerance-checked"):
            i = dve.reciprocal(rqk_b[:, :], r48f[:, :])
        c_dve.inc(i)
        rms_done = c_dve.n
        bcb = [f6, f7]
        for p in range(16):
            which = "q" if p % 2 == 0 else "k"
            m = p // 2
            if which == "q":
                rsrc, csb = rqk_b[0:16, :], cselb[0:16, :]
            else:
                rsrc, csb = rqk_b[32:48, :], cselb[32:48, :]
            bank = bcb[p % 2]
            if p < 2:
                pe.wait_ge(c_dve.h, rms_done)
                if p == 0:
                    pe.wait_ge(c_dve.h, carry["ynorm15"])
                else:
                    pe.wait_ge(c_dve.h, carry["a1_copy_last"])
            else:
                pe.wait_ge(c_dve.h, ev[("app", which, m - 1)])
            i = pe.matmul(bank[:, :], csb[:, m * 128:(m + 1) * 128],
                          rsrc, start=True, stop=True)
            c_pe.inc(i)
            dve.wait_ge(c_pe.h, c_pe.n)
            if which == "q":
                if p == 0:
                    dve.wait_ge(c_pe.h, carry["smm_last"])
                dst = hTq[:, m * 512:(m + 1) * 512]
                src0 = qyh[:, 4096 + m * 512: 4096 + (m + 1) * 512]
                i = dve.scalar_tensor_tensor(dst, src0, 0.125, bank[:, :],
                                             ALU.mult, ALU.mult)
            else:
                win = kTact[:, m * 2048 + g * 512: m * 2048 + (g + 1) * 512]
                i = dve.scalar_tensor_tensor(win, win, 1.0, bank[:, :],
                                             ALU.mult, ALU.mult)
            c_dve.inc(i)
            ev[("app", which, m)] = c_dve.n
        carry["a2_apply_last"] = c_dve.n

    def emit_C(s_i, g):
        tbase = s_i * NT_SEQ + 4 * g
        act.preload_act_table(AF.Sqrt)
        dve.wait_ge(sem_x2w.h, sem_x2w.n)
        dve.wait_ge(sem_hw.h, sem_hw.n)
        dve.wait_ge(sem_h2w.h, sem_h2w.n)
        dve.wait_ge(c_pe.h, carry["a1_mm_last"])
        for tl in range(4):
            ti = tbase + tl
            for half in range(2):
                bank = s01[:, half * 512:(half + 1) * 512]
                if tl >= 1:
                    pe.wait_ge(c_dve.h, ev[("x2", tl - 1, half)])
                else:
                    pe.wait_ge(c_dve.h, carry["ynorm15"])
                    pe.wait_ge(c_act.h, carry["tanh_s01"])
                for di in range(8):
                    i = pe.matmul(
                        bank,
                        qyh[:, di * 512 + tl * 128: di * 512 + (tl + 1) * 128],
                        woT[:, di * 1024 + half * 512: di * 1024 + (half + 1) * 512],
                        start=(di == 0), stop=(di == 7))
                c_pe.inc(i)
                ev[("omm", tl, half)] = c_pe.n
            if tl >= 1:
                dve.wait_ge(sem_x2w.h, sem_x2w.n)
            dve.drain()
            x2slot = [x2b[:, :], htm[:, 0:1024], htm[:, 1024:2048], sqjunk[:, :]][tl]
            for half in range(2):
                bank = s01[:, half * 512:(half + 1) * 512]
                dve.wait_ge(c_pe.h, ev[("omm", tl, half)])
                i = dve.tensor_tensor(
                    x2slot[:, half * 512:(half + 1) * 512], bank,
                    xb[:, (ti % 8) * 1024 + half * 512:(ti % 8) * 1024 + (half + 1) * 512],
                    ALU.add)
                c_dve.inc(i)
                ev[("x2", tl, half)] = c_dve.n
            gp.wait_ge(c_dve.h, ev[("x2", tl, 1)])
            i = gp.dma_start(x2_s[ti * 128:(ti + 1) * 128, :], x2slot)
            sem_x2w.inc(i)
        slots = [x2b[:, :], htm[:, 0:1024], htm[:, 1024:2048], sqjunk[:, :]]
        lane = lambda l: sln[:, :].rearrange("p (t l) -> p t l", l=8)[:, :, l:l + 1]
        dve.drain()
        for tl in range(4):
            dve.tensor_scalar(pTb[:, 0:1024], slots[tl], 1.0, 0.0, ALU.mult,
                              ALU.add, accum_out=sln[:, 8 * tl: 8 * tl + 1])
            dve.drain()
            dve.scalar_tensor_tensor(pTb[:, 0:1024], slots[tl], 1.0, slots[tl],
                                     ALU.mult, ALU.mult,
                                     accum_out=sln[:, 8 * tl + 1: 8 * tl + 2])
            dve.drain()
        dve.tensor_scalar(lane(2), lane(0), 1.0 / D, None, ALU.mult)
        dve.tensor_scalar(lane(3), lane(1), 1.0 / D, None, ALU.mult)
        dve.drain()
        dve.scalar_tensor_tensor(lane(4), lane(2), 1.0, lane(2), ALU.mult, ALU.mult)
        dve.drain()
        i = dve.tensor_tensor(lane(5), lane(3), lane(4), ALU.subtract)
        c_dve.inc(i)
        act.wait_ge(c_dve.h, c_dve.n)
        i = act.activation(lane(6), lane(5), AF.Sqrt, bias=cconst[:, 0:1], scale=1.0)
        c_act.inc(i)
        dve.wait_ge(c_act.h, c_act.n)
        dve.reciprocal(lane(6), lane(6))
        dve.drain()
        dve.tensor_tensor(lane(7), lane(2), lane(6), ALU.mult)
        dve.drain()
        dve.wait_ge(sem_x2w.h, sem_x2w.n)
        for tl in range(4):
            ti = tbase + tl
            i = dve.tensor_scalar(slots[tl], slots[tl],
                                  sln[:, 8 * tl + 6: 8 * tl + 7],
                                  sln[:, 8 * tl + 7: 8 * tl + 8],
                                  ALU.mult, ALU.subtract)
            n = c_dve.inc(i)
            gp.wait_ge(c_dve.h, n)
            i = gp.dma_start(h2_s[ti * 128:(ti + 1) * 128, :], slots[tl])
            sem_h2w.inc(i)
        carry["omm_last"] = ev[("omm", 3, 1)]
        carry["po_free"] = {0: ev[("x2", 3, 0)], 1: ev[("x2", 3, 1)]}
        carry["ln2_last"] = c_dve.n
        carry["x2_res_last"] = ev[("x2", 3, 1)]
        if g + 2 < NG:
            gp.wait_ge(c_dve.h, ev[("x2", 3, 1)])
            for tl in range(4):
                ti = tbase + 8 + tl
                if ti not in x_done:
                    load_x(ti)

    def emit_B(s_i, g, next_g, extra=()):
        tbase = s_i * NT_SEQ + 4 * g
        a1_units = []
        act.preload_act_table(AF.Exp)
        items = [("pair", kt0) for kt0 in range(0, 4 * g + 4, 2)]
        n_items = len(items)
        n_b = H * n_items
        n_units = (24 if next_g is not None else 0) + len(extra)
        inject_every = max(1, (n_b - 8) // (n_units + 1)) if n_units else 10 ** 9
        spair = [s01, s23]
        tanh_keys = ["tanh_s01", "tanh_s23"]
        last_pv_on_slot = carry["pv_slot"]

        def emit_normA(h_i):
            pyb = pyv[h_i % 2]
            dve.wait_ge(c_pe.h, ev[("pv_last", h_i)])
            with nc.allow_low_precision(reason="softmax denom recip bf16; tolerance-checked"):
                i = dve.reciprocal(recb, pyb[64:65, :])
            c_dve.inc(i)
            ev[("recb", h_i)] = c_dve.n

        def emit_normB(h_i):
            chunk, part = h_i // 2, (h_i % 2) * 64
            pyb = pyv[h_i % 2]
            pe.wait_ge(c_dve.h, ev[("recb", h_i)])
            if h_i >= 1:
                pe.wait_ge(c_dve.h, ev[("ynorm", h_i - 1)])
            else:
                pe.wait_ge(c_dve.h, carry["a2f6_free"])
            i = pe.matmul(f6[0:64, :], cones[:, :], recb,
                          start=True, stop=True)
            c_pe.inc(i)
            ev[("ybc", h_i)] = c_pe.n
            dve.wait_ge(c_pe.h, ev[("ybc", h_i)])
            dve.tensor_copy(rb[:, :], f6[0:64, :])
            dve.drain()
            i = dve.tensor_tensor(
                qyh[part:part + 64, chunk * 512:(chunk + 1) * 512],
                pyb[0:64, :], rb[:, :], ALU.mult)
            c_dve.inc(i)
            ev[("ynorm", h_i)] = c_dve.n

        def emit_S(h_i, it_i):
            chunk, part = h_i // 2, (h_i % 2) * 64
            kind, kt0 = items[it_i]
            sp_i = (h_i * n_items + it_i) % 2
            sb = spair[sp_i]
            if carry[tanh_keys[sp_i]] is not None:
                pe.wait_ge(c_act.h, carry[tanh_keys[sp_i]])
            if h_i == 0 and it_i < 2:
                pe.wait_ge(c_dve.h, carry["a2_apply_last"])
                if sp_i == 0:
                    pe.wait_ge(c_dve.h, carry["ln2_last"])
                else:
                    pe.wait_ge(c_pe.h, carry["sel_last"])
            for u in range(2):
                kt = kt0 + u
                i = pe.matmul(
                    sb[0:128, u * 512:(u + 1) * 512],
                    kTact[part:part + 64,
                          chunk * 2048 + kt * 128: chunk * 2048 + (kt + 1) * 128],
                    hTq[part:part + 64,
                        chunk * 512:(chunk + 1) * 512],
                    start=True, stop=True)
            c_pe.inc(i)
            ev[("smm", h_i, it_i)] = c_pe.n
            act.wait_ge(c_pe.h, ev[("smm", h_i, it_i)])
            if last_pv_on_slot[sp_i] is not None:
                act.wait_ge(c_pe.h, last_pv_on_slot[sp_i])
            off0 = (kt0 - 4 * g) * 128 if kt0 >= 4 * g else 0
            i = act.activation(tanhsil[:, sp_i * 1024 + off0: (sp_i + 1) * 1024],
                               sb[0:128, off0:1024],
                               AF.Tanh, scale=1.0 / SOFTCAP)
            c_act.inc(i)
            carry[tanh_keys[sp_i]] = c_act.n
            act.drain()
            i = act.activation(pTb[:, sp_i * 1024 + off0: (sp_i + 1) * 1024],
                               tanhsil[:, sp_i * 1024 + off0: (sp_i + 1) * 1024],
                               AF.Exp, scale=SOFTCAP)
            c_act.inc(i)
            ev[("exp", h_i, it_i)] = c_act.n
            for u in range(2):
                kt = kt0 + u
                if kt >= 4 * g:
                    j = kt - 4 * g
                    off = sp_i * 1024 + u * 512 + j * 128
                    gp.wait_ge(c_act.h, ev[("exp", h_i, it_i)])
                    i = gp.tensor_tensor(pTb[:, off: off + 128],
                                         pTb[:, off: off + 128],
                                         cut[:, :], ALU.mult)
                    c_gp.inc(i)
                    ev[("mask", h_i, it_i, u)] = c_gp.n

        def emit_PV(h_i, it_i):
            pyb = pyv[h_i % 2]
            kind, kt0 = items[it_i]
            sp_i = (h_i * n_items + it_i) % 2
            if it_i == 0:
                if h_i >= 2:
                    pe.wait_ge(c_dve.h, ev[("ynorm", h_i - 2)])
                else:
                    pe.wait_ge(c_dve.h, carry["b_dve_end"])
            for u in range(2):
                kt = kt0 + u
                j = kt - 4 * g
                qoff = j * 128 if j > 0 else 0
                if kt >= 4 * g:
                    pe.wait_ge(c_gp.h, ev[("mask", h_i, it_i, u)])
                elif u == 0:
                    pe.wait_ge(c_act.h, ev[("exp", h_i, it_i)])
                i = pe.matmul(
                    pyb[0:65, qoff:512],
                    vbuf[:, kt * 1040 + h_i * 65: kt * 1040 + (h_i + 1) * 65],
                    pTb[:, sp_i * 1024 + u * 512 + qoff: sp_i * 1024 + (u + 1) * 512],
                    start=(it_i == 0 and u == 0),
                    stop=(it_i == n_items - 1 and u == 1))
                c_pe.inc(i)
            last_pv_on_slot[sp_i] = c_pe.n
            if it_i == n_items - 1:
                ev[("pv_last", h_i)] = c_pe.n

        stream = [(h_i, it_i) for h_i in range(H) for it_i in range(n_items)]
        a1_i = 0
        for idx, (h_i, it_i) in enumerate(stream):
            emit_S(h_i, it_i)
            if idx >= 1:
                emit_PV(*stream[idx - 1])
            if it_i == 0 and h_i >= 1:
                emit_normA(h_i - 1)
            if it_i == min(2, n_items - 1) and h_i >= 1:
                emit_normB(h_i - 1)
            if n_units:
                if idx == 2 and next_g is not None:
                    emit_ln1_part2(s_i, next_g, inside_b=True)
                elif idx == 5:
                    if next_g is not None:
                        emit_hT_load(s_i, next_g)
                        a1_units = make_a1_units(s_i, next_g) + list(extra)
                    else:
                        a1_units = list(extra)
                elif idx >= 8:
                    while a1_i < len(a1_units) and \
                            a1_i < (idx - 7) // inject_every + 1 and \
                            (idx - 7) % inject_every == 0 and a1_i <= (idx - 7) // inject_every:
                        a1_units[a1_i]()
                        a1_i += 1
        emit_PV(*stream[-1])
        while a1_i < len(a1_units):
            a1_units[a1_i]()
            a1_i += 1
        emit_normA(H - 1)
        emit_normB(H - 1)
        carry["smm_last"] = ev[("smm", H - 1, n_items - 1)]
        carry["pv_last_final"] = ev[("pv_last", H - 1)]
        carry["ynorm15"] = ev[("ynorm", H - 1)]
        carry["b_act_end"] = c_act.n
        carry["b_dve_end"] = c_dve.n

    pre_done = set()
    for s_i in range(SEQ_PER_CORE):
        act.preload_act_table(AF.Sqrt)
        vv = vbuf[:, :].rearrange("p (gg dd) -> p gg dd", dd=65)
        c_dve.inc(dve.memset(vv[:, :, 64:65], 1.0))
        for tl in range(4):
            ti = s_i * NT_SEQ + 4 + tl
            if ti not in x_done:
                load_x(ti)
        carry["a2f6_free"] = 0
        if s_i not in pre_done:
            emit_ln1_part1(s_i, 0)
            emit_ln1_part2(s_i, 0)
            emit_hT_load(s_i, 0)
        for u in make_a1_units(s_i, 0, standalone=True):
            u()
        emit_A2(s_i, 0)
        carry["a2f6_free"] = carry["a2_apply_last"]
        for g in range(NG):
            if g > 0:
                emit_A2(s_i, g)
                carry["a2f6_free"] = carry["a2_apply_last"]
                emit_C(s_i, g - 1)
            if g + 1 < NG:
                emit_ln1_part1(s_i, g + 1)
            if s_i == 0:
                lo, hi = FFN_SPLIT[g], FFN_SPLIT[g + 1]
                extra = ffn_steps[lo:hi]
            else:
                extra = ()
            emit_B(s_i, g, g + 1 if g + 1 < NG else None, extra=extra)
        emit_C(s_i, NG - 1)
        if s_i + 1 < SEQ_PER_CORE:
            gp.wait_ge(c_dve.h, carry["x2_res_last"])
            for tl in range(4):
                ti = (s_i + 1) * NT_SEQ + tl
                if ti not in x_done:
                    load_x(ti)
            emit_ln1_part1(s_i + 1, 0)
            emit_ln1_part2(s_i + 1, 0)
            emit_hT_load(s_i + 1, 0)
            pre_done.add(s_i + 1)

        nc.all_engine_barrier()
        act.preload_act_table(AF.Silu)
        sp.wait_ge(sem_h2w.h, sem_h2w.n)
        gp.wait_ge(sem_x2w.h, sem_x2w.n)
        sp.wait_ge(sem_fpp[0].h, sem_fpp[0].n)
        sp.wait_ge(sem_fpp[1].h, sem_fpp[1].n)
        if s_i + 1 < SEQ_PER_CORE:
            for tl in range(4):
                if (s_i + 1) * NT_SEQ + tl not in x_done:
                    load_x((s_i + 1) * NT_SEQ + tl)
        last_out_dma = {0: None, 1: None}

        def emit_h2r(tc, dis=range(8)):
            hslot = (tc % 2) * 4096
            for di in dis:
                i = sp.dma_start(
                    qyh[:, hslot + di * 512: hslot + (di + 1) * 512],
                    h2_s[s_i * S + tc * 512: s_i * S + (tc + 1) * 512,
                         di * 128:(di + 1) * 128],
                    transpose=True)
                sem_h2r[tc % 2].inc(i)
            ev[("h2r", tc)] = sem_h2r[tc % 2].n

        for tc in range(4):
            slot = (tc % 2) * 4096
            if tc == 0:
                emit_h2r(0)
            for tl in range(4):
                ti_seq = tc * 4 + tl
                if tc >= 1:
                    gp.wait_ge(c_dve.h, ev[("outD", tc - 1, 1, tl)])
                i = gp.dma_start(
                    xb[:, (4 + tl) * 1024:(5 + tl) * 1024],
                    x2_s[s_i * S + ti_seq * 128: s_i * S + (ti_seq + 1) * 128, :])
                sem_x2r.inc(i)
            ev[("x2r", tc)] = sem_x2r.n

            if tc > 0 or s_i > 0:
                sp.wait_ge(c_pe.h, ev.get(("dmm_prev",), 0))
            i = sp.dma_start(
                vbuf[:, 0:12288],
                wdt2_s[0].rearrange("p c d -> p (c d)"))
            sem_fwd.inc(i)
            for hc in range(NHC):
                wslot = (hc % 2) * 1024
                if hc >= 2:
                    sp.wait_ge(c_pe.h, ev[("gu", tc, hc - 2)])
                sfw = sem_fw[hc % 2]
                sfw.inc(sp.dma_start(wgub[:, wslot:wslot + 1024], wgt_s[hc, :, :]))
                sfw.inc(sp.dma_start(wuub[:, wslot:wslot + 1024], wut_s[hc, :, :]))
                ev[("fw", tc, hc)] = sfw.n
                pe.wait_ge(sfw.h, ev[("fw", tc, hc)])
                if hc == 0:
                    pe.wait_ge(sem_h2r[tc % 2].h, ev[("h2r", tc)])
                    if s_i == 0 and tc == 0:
                        pe.wait_ge(sem_fpp[0].h, sem_fpp[0].n)
                        pe.wait_ge(sem_fpp[1].h, sem_fpp[1].n)
                gbank = pqv[0] if hc % 2 == 0 else f6[:, :]
                ubank = pqv[1] if hc % 2 == 0 else f7[:, :]
                if hc >= 2:
                    pe.wait_ge(c_act.h, ev[("silu", tc, hc - 2)])
                for di in range(8):
                    i = pe.matmul(gbank,
                                  wgub[:, wslot + di * 128: wslot + (di + 1) * 128],
                                  qyh[:, slot + di * 512: slot + (di + 1) * 512],
                                  start=(di == 0), stop=(di == 7))
                c_pe.inc(i)
                ev[("gmm", tc, hc)] = c_pe.n
                if hc >= 2:
                    pe.wait_ge(c_dve.h, ev[("actT", tc, hc - 2)])
                for di in range(8):
                    i = pe.matmul(ubank,
                                  wuub[:, wslot + di * 128: wslot + (di + 1) * 128],
                                  qyh[:, slot + di * 512: slot + (di + 1) * 512],
                                  start=(di == 0), stop=(di == 7))
                c_pe.inc(i)
                ev[("gu", tc, hc)] = c_pe.n
                act.wait_ge(c_pe.h, ev[("gmm", tc, hc)])
                if hc >= 2:
                    act.wait_ge(c_dve.h, ev[("actT", tc, hc - 2)])
                i = act.activation(tanhsil[:, (hc % 2) * 512:(hc % 2 + 1) * 512],
                                   gbank, AF.Silu, scale=DEQ)
                c_act.inc(i)
                ev[("silu", tc, hc)] = c_act.n
                dve.wait_ge(c_act.h, ev[("silu", tc, hc)])
                dve.wait_ge(c_pe.h, ev[("gu", tc, hc)])
                if tc >= 1 and hc == 0:
                    dve.wait_ge(c_pe.h, ev[("dmm", tc - 1, 1, 3)])
                i = dve.scalar_tensor_tensor(
                    kTact[:, hc * 512:(hc + 1) * 512], ubank, DEQ,
                    tanhsil[:, (hc % 2) * 512:(hc % 2 + 1) * 512],
                    ALU.mult, ALU.mult)
                c_dve.inc(i)
                ev[("actT", tc, hc)] = c_dve.n
                if hc >= 8 and hc % 2 == 0 and tc + 1 < 4:
                    di = (hc - 8) // 2
                    if di == 0 and tc >= 1:
                        sp.wait_ge(c_pe.h, ev[("guD", tc - 1)])
                    emit_h2r(tc + 1, dis=[di])
            ev[("guD", tc)] = c_pe.n

            for half in range(2):
                if half == 1:
                    sp.wait_ge(c_pe.h, ev[("dmm", tc, 0, 3)])
                    i = sp.dma_start(
                        vbuf[:, 0:12288],
                        wdt2_s[1].rearrange("p c d -> p (c d)"))
                    sem_fwd.inc(i)
                pe.wait_ge(sem_fwd.h, sem_fwd.n)
                pe.wait_ge(c_dve.h, ev[("actT", tc, NHC - 1)])
                for tl in range(4):
                    bank = pdv[tl]
                    if half == 1:
                        pe.wait_ge(c_dve.h, ev[("outD", tc, 0, tl)])
                    elif tc >= 1:
                        pe.wait_ge(c_dve.h, ev[("outD", tc - 1, 1, tl)])
                    for hc in range(NHC):
                        i = pe.matmul(bank[:, :],
                                      kTact[:, hc * 512 + tl * 128: hc * 512 + (tl + 1) * 128],
                                      vbuf[:, hc * 512:(hc + 1) * 512],
                                      start=(hc == 0), stop=(hc == NHC - 1))
                    c_pe.inc(i)
                    ev[("dmm", tc, half, tl)] = c_pe.n
                    ev[("dmm_prev",)] = c_pe.n
                for tl in range(4):
                    bank = pdv[tl]
                    dve.wait_ge(c_pe.h, ev[("dmm", tc, half, tl)])
                    dve.wait_ge(sem_x2r.h, ev[("x2r", tc)])
                    if sem_out.n > 0:
                        dve.wait_ge(sem_out.h, sem_out.n)
                    i = dve.scalar_tensor_tensor(
                        outb[:, :], bank[:, :], DEQ,
                        xb[:, (4 + tl) * 1024 + half * 512:(4 + tl) * 1024 + (half + 1) * 512],
                        ALU.mult, ALU.add)
                    c_dve.inc(i)
                    ev[("outD", tc, half, tl)] = c_dve.n
                    gp.wait_ge(c_dve.h, ev[("outD", tc, half, tl)])
                    ti = s_i * NT_SEQ + tc * 4 + tl
                    i = gp.dma_start(
                        out_h[ti * 128:(ti + 1) * 128, half * 512:(half + 1) * 512],
                        outb[:, :])
                    sem_out.inc(i)
                    last_out_dma[half] = sem_out.n

        nc.all_engine_barrier()

    gp.wait_ge(sem_out.h, sem_out.n)
    es.close()
    return nc


def _make_consts():
    cut = np.triu(np.ones((128, 128), np.float32)).astype(BF16)
    csel = np.zeros((128, 8, 16), np.float32)
    for m in range(8):
        for p in range(128):
            csel[p, m, 2 * m + p // 64] = 1.0
    csel = csel.reshape(128, 128).astype(BF16)
    cselb = np.zeros((16, 8, 128), np.float32)
    for m in range(8):
        for p in range(128):
            cselb[2 * m + p // 64, m, p] = 1.0
    cselb = cselb.reshape(16, 1024).astype(BF16)
    cones = np.ones((1, 64), np.float32).astype(BF16)
    cconst = np.zeros((128, 4), np.float32)
    cconst[:, 0] = EPS_LN
    cconst[:, 1] = EPS_RMS
    cid = np.eye(128, dtype=np.float32).astype(BF16)
    return cut, csel, cselb, cones, cconst, cid


_NC_CACHE = {}


def kernel(L0_attn_x, L0_attn_w_q, L0_attn_w_k, L0_attn_w_v, L0_attn_w_o,
           L0_ffn_w_gate, L0_ffn_w_up, L0_ffn_w_down):
    x = np.ascontiguousarray(np.asarray(L0_attn_x, dtype=np.float32)).reshape(B, S, D)
    wq = np.ascontiguousarray(np.asarray(L0_attn_w_q, dtype=np.float32))
    wk = np.ascontiguousarray(np.asarray(L0_attn_w_k, dtype=np.float32))
    wv = np.ascontiguousarray(np.asarray(L0_attn_w_v, dtype=np.float32))
    wo = np.ascontiguousarray(np.asarray(L0_attn_w_o, dtype=np.float32))
    wg = np.ascontiguousarray(np.asarray(L0_ffn_w_gate, dtype=np.int32))
    wu = np.ascontiguousarray(np.asarray(L0_ffn_w_up, dtype=np.int32))
    wd = np.ascontiguousarray(np.asarray(L0_ffn_w_down, dtype=np.int32))
    cut, csel, cselb, cones, cconst, cid = _make_consts()

    if "nc" not in _NC_CACHE:
        _NC_CACHE["nc"] = build_nc()
    nc = _NC_CACHE["nc"]

    in_maps = []
    for c in range(NC):
        shard = np.ascontiguousarray(
            x[c * SEQ_PER_CORE:(c + 1) * SEQ_PER_CORE].reshape(TOK, D))
        in_maps.append({
            "x": shard, "wq": wq, "wk": wk, "wv": wv, "wo": wo,
            "wg": wg, "wu": wu, "wd": wd,
            "cut": cut, "csel": csel, "cselb": cselb,
            "cones": cones, "cconst": cconst, "cid": cid,
        })
    res = run_bass_kernel_spmd(nc, in_maps, core_ids=list(range(NC)))
    outs = [np.asarray(res.results[c]["out"], dtype=np.float32).reshape(
        SEQ_PER_CORE, S, D) for c in range(NC)]
    return np.concatenate(outs, axis=0)


# version 19
# speedup vs baseline: 1.0160x; 1.0160x over previous
import contextlib
import numpy as np
import ml_dtypes
import concourse.bass as bass
import concourse.mybir as mybir
from concourse.bass_utils import run_bass_kernel_spmd

BF16 = ml_dtypes.bfloat16
F32 = mybir.dt.float32
BF = mybir.dt.bfloat16
I32 = mybir.dt.int32
AF = mybir.ActivationFunctionType
ALU = mybir.AluOpType

B, S, D = 16, 2048, 1024
H, DK = 16, 64
HID = 3072
NC = 8
SEQ_PER_CORE = B // NC
TOK = SEQ_PER_CORE * S
NT_SEQ = S // 128
NG = 4
FFN_SPLIT = [0, 4, 20, 44, 72]
NHC = HID // 128
EPS_LN = 1e-5
EPS_RMS = 1.1920929e-07
SOFTCAP = 50.0
DEQ = 1.0 / 127.0


class Ctr:
    def __init__(self, h, step=1):
        self.h = h
        self.n = 0
        self.step = step

    def inc(self, instr, k=None):
        k = self.step if k is None else k
        instr.then_inc(self.h, k)
        self.n += k
        return self.n


def build_nc(stop=None, dbg=()):
    nc = bass.Bass("TRN2", target_bir_lowering=False)

    x_h = nc.dram_tensor("x", [TOK, D], F32, kind="ExternalInput")
    wq_h = nc.dram_tensor("wq", [D, D], F32, kind="ExternalInput")
    wk_h = nc.dram_tensor("wk", [D, D], F32, kind="ExternalInput")
    wv_h = nc.dram_tensor("wv", [D, D], F32, kind="ExternalInput")
    wo_h = nc.dram_tensor("wo", [D, D], F32, kind="ExternalInput")
    wg_h = nc.dram_tensor("wg", [HID, D], I32, kind="ExternalInput")
    wu_h = nc.dram_tensor("wu", [HID, D], I32, kind="ExternalInput")
    wd_h = nc.dram_tensor("wd", [D, HID], I32, kind="ExternalInput")
    cut_h = nc.dram_tensor("cut", [128, 128], BF, kind="ExternalInput")
    csel_h = nc.dram_tensor("csel", [128, 128], BF, kind="ExternalInput")
    cselb_h = nc.dram_tensor("cselb", [16, 1024], BF, kind="ExternalInput")
    cones_h = nc.dram_tensor("cones", [1, 64], BF, kind="ExternalInput")
    cconst_h = nc.dram_tensor("cconst", [128, 4], F32, kind="ExternalInput")
    cid_h = nc.dram_tensor("cid", [128, 128], BF, kind="ExternalInput")
    out_h = nc.dram_tensor("out", [TOK, D], F32, kind="ExternalOutput")

    wg_s = nc.dram_tensor("wg_s", [HID, D], BF, kind="Internal")
    wu_s = nc.dram_tensor("wu_s", [HID, D], BF, kind="Internal")
    wd_s = nc.dram_tensor("wd_s", [D, HID], BF, kind="Internal")
    wgt_s = nc.dram_tensor("wgt_s", [NHC, 128, 1024], BF, kind="Internal")
    wut_s = nc.dram_tensor("wut_s", [NHC, 128, 1024], BF, kind="Internal")
    wdt2_s = nc.dram_tensor("wdt2_s", [2, 128, NHC, 512], BF, kind="Internal")
    h_s = nc.dram_tensor("h_s", [TOK, D], BF, kind="Internal")
    h2_s = nc.dram_tensor("h2_s", [TOK, D], BF, kind="Internal")
    x2_s = nc.dram_tensor("x2_s", [TOK, D], BF, kind="Internal")

    es = contextlib.ExitStack()
    sbt = lambda name, shape, dt: es.enter_context(nc.sbuf_tensor(name, shape, dt))
    pst = lambda name, shape, dt: es.enter_context(nc.psum_tensor(name, shape, dt))
    mks = lambda name: es.enter_context(nc.semaphore(name))

    wqT = sbt("wqT", [128, 8192], BF)
    wkT = sbt("wkT", [128, 8192], BF)
    wvT = sbt("wvT", [128, 8192], BF)
    woT = sbt("woT", [128, 8192], BF)
    kTact = sbt("kTact", [128, 16384], BF)
    vbuf = sbt("vbuf", [128, 16640], BF)
    xb = sbt("xb", [128, 8192], BF)
    hTq = sbt("hTq", [128, 4096], BF)
    qyh = sbt("qyh", [128, 8192], BF)
    htm = sbt("htm", [128, 2048], BF)
    x2b = sbt("x2b", [128, 1024], BF)
    tanhsil = sbt("tanhsil", [128, 2048], F32)
    pTb = sbt("pTb", [128, 2048], BF)
    sqjunk = sbt("sqjunk", [128, 1024], BF)
    outb = sbt("outb", [128, 512], F32)
    wgub = sbt("wgub", [128, 2048], BF)
    wuub = sbt("wuub", [128, 2048], BF)
    sln = sbt("sln", [128, 32], F32)
    r48f = sbt("r48f", [48, 512], F32)
    rqk_b = sbt("rqk_b", [48, 512], BF)
    recb = rqk_b[0:1, :]
    rb = sbt("rb", [64, 512], BF)
    cut = sbt("cutb", [128, 128], BF)
    csel = sbt("cself", [128, 128], BF)
    cselb = sbt("cselbb", [48, 1024], BF)
    cones = sbt("conesb", [1, 64], BF)
    cconst = sbt("cconstb", [128, 4], F32)
    cid = sbt("cidb", [128, 128], BF)

    s01 = pst("s01", [128, 1024], F32)
    s23 = pst("s23", [128, 1024], F32)
    f4 = pst("f4", [128, 512], F32)
    f5 = pst("f5", [128, 512], F32)
    f6 = pst("f6", [128, 512], F32)
    f7 = pst("f7", [128, 512], F32)
    pqv = [s01[:, 0:512], s01[:, 512:1024]]
    selv = s23[:, 0:512]
    bcv = [f4, f5]
    pov = [f6, f7]
    pdv = [s23[:, 0:512], s23[:, 512:1024], f4[:, :], f5[:, :]]
    spair = [s01, s23]
    pyv = [f4, f5]
    ybcv = [f6, f7]

    sem_x = [Ctr(mks("sem_x0"), 16), Ctr(mks("sem_x1"), 16)]
    sem_wcf = Ctr(mks("sem_wcf"), 16)
    sem_wl = [Ctr(mks("sem_wl0"), 16), Ctr(mks("sem_wl1"), 16)]
    sem_cst = Ctr(mks("sem_cst"), 16)
    sem_fpp = [Ctr(mks("sem_fpp0"), 16), Ctr(mks("sem_fpp1"), 16)]
    sem_hw = Ctr(mks("sem_hw"), 16)
    sem_hr = Ctr(mks("sem_hr"), 16)
    sem_h2w = Ctr(mks("sem_h2w"), 16)
    sem_h2r = [Ctr(mks("sem_h2r0"), 16), Ctr(mks("sem_h2r1"), 16)]
    sem_x2w = Ctr(mks("sem_x2w"), 16)
    sem_x2r = Ctr(mks("sem_x2r"), 16)
    sem_fw = [Ctr(mks("sem_fw0"), 16), Ctr(mks("sem_fw1"), 16)]
    sem_fwd = Ctr(mks("sem_fwd"), 16)
    sem_out = Ctr(mks("sem_out"), 16)
    c_pe = Ctr(mks("c_pe"))
    c_gp = Ctr(mks("c_gp"))
    c_act = Ctr(mks("c_act"))
    c_dve = Ctr(mks("c_dve"))

    pe, act, dve, gp, sp = nc.tensor, nc.scalar, nc.vector, nc.gpsimd, nc.sync

    x_done = {}

    def load_x(ti):
        sx = sem_x[(ti // 4) % 2]
        i = gp.dma_start(xb[:, (ti % 8) * 1024:(ti % 8 + 1) * 1024],
                         x_h[ti * 128:(ti + 1) * 128, :])
        sx.inc(i)
        x_done[ti] = True

    for dst, src in ((cut[:, :], cut_h), (csel[:, :], csel_h),
                     (cselb[0:16, :], cselb_h), (cselb[32:48, :], cselb_h),
                     (cones[:, :], cones_h), (cconst[:, :], cconst_h),
                     (cid[:, :], cid_h)):
        sem_cst.inc(sp.dma_start(dst, src[:, :]))
    consts_done = sem_cst.n

    for ti in range(8):
        load_x(ti)
    for nm, (src, dst) in dict(
            g=(wg_h, wg_s), u=(wu_h, wu_s), d=(wd_h, wd_s)).items():
        R = dst.shape[0] // 16
        for c in range(16):
            sem_wcf.inc(gp.dma_start(dst[c * R:(c + 1) * R, :],
                                     src[c * R:(c + 1) * R, :]))

    i = dve.memset(r48f[0:32, :], 1.0)
    c_dve.inc(i)
    r48f_init = c_dve.n
    act.wait_ge(sem_cst.h, consts_done)
    act.wait_ge(c_dve.h, r48f_init)
    dve.wait_ge(sem_cst.h, consts_done)
    gp.wait_ge(sem_cst.h, consts_done)
    pe.wait_ge(sem_cst.h, consts_done)

    cp_ev = {}
    tp_ev = {}
    ktact_mm_last = {}
    def emit_wT(wi, src, dst):
        half = (wi % 2) * 8192
        if wi >= 2:
            sp.wait_ge(c_pe.h, ktact_mm_last[wi - 2])
        for ro in range(8):
            t = wi * 8 + ro
            sl = tanhsil[:, (t % 2) * 1024:(t % 2 + 1) * 1024]
            if t >= 2:
                sp.wait_ge(c_dve.h, cp_ev[t - 2])
            sem_wl[t % 2].inc(sp.dma_start(sl, src[ro * 128:(ro + 1) * 128, :]))
            dve.wait_ge(sem_wl[t % 2].h, sem_wl[t % 2].n)
            i = dve.tensor_copy(kTact[:, half + ro * 1024: half + (ro + 1) * 1024], sl)
            c_dve.inc(i)
            cp_ev[t] = c_dve.n
        pe.wait_ge(c_dve.h, cp_ev[wi * 8 + 7])
        for di in range(8):
            tb = wi * 8 + di
            b0, b1 = (f6, f7) if tb % 2 == 0 else (f4, f5)
            if tb >= 2:
                pe.wait_ge(c_dve.h, tp_ev[tb - 2])
            for qo in range(8):
                bank = b0 if qo < 4 else b1
                i = pe.matmul(bank[:, (qo % 4) * 128:(qo % 4 + 1) * 128],
                              kTact[:, half + qo * 1024 + di * 128:
                                    half + qo * 1024 + (di + 1) * 128],
                              cid[:, :], start=True, stop=True)
            c_pe.inc(i)
            dve.wait_ge(c_pe.h, c_pe.n)
            dve.tensor_copy(dst[:, di * 1024: di * 1024 + 512], b0[:, :])
            i = dve.tensor_copy(dst[:, di * 1024 + 512:(di + 1) * 1024], b1[:, :])
            c_dve.inc(i)
            tp_ev[tb] = c_dve.n
        ktact_mm_last[wi] = c_pe.n
    for wi, (src, dst) in enumerate(
            [(wq_h, wqT), (wk_h, wkT), (wv_h, wvT), (wo_h, woT)]):
        emit_wT(wi, src, dst)
    attn_w_done = c_dve.n

    fpp_done = {}
    ffn_steps = []
    def mk_ffn_step(src, dst2, h0, di, hc, pidx, is_wd):
        def go():
            buf = wgub if pidx % 2 == 0 else wuub
            par = pidx % 2
            if pidx == 0:
                sp.wait_ge(sem_wcf.h, sem_wcf.n)
            if pidx >= 2:
                sp.wait_ge(sem_fpp[par].h, fpp_done[pidx - 2])
            if is_wd:
                sem_fpp[par].inc(sp.dma_start(
                    buf[:, 0:1024], src[0:1024, hc * 128:(hc + 1) * 128],
                    transpose=True))
                sp.wait_ge(sem_fpp[par].h, sem_fpp[par].n)
                sem_fpp[par].inc(sp.dma_start(
                    dst2[0, :, hc, :], buf[:, 0:512]))
                sem_fpp[par].inc(sp.dma_start(
                    dst2[1, :, hc, :], buf[:, 512:1024]))
            else:
                sem_fpp[par].inc(sp.dma_start(
                    buf[:, 0:1024], src[h0:h0 + 1024, di * 128:(di + 1) * 128],
                    transpose=True))
                sp.wait_ge(sem_fpp[par].h, sem_fpp[par].n)
                sem_fpp[par].inc(sp.dma_start(
                    dst2[h0 // 128:h0 // 128 + 8, :, di * 128:(di + 1) * 128]
                    .rearrange("c p m -> p c m"),
                    buf[:, 0:1024].rearrange("p (c m) -> p c m", m=128)))
            fpp_done[pidx] = sem_fpp[par].n
        return go
    pidx = 0
    for src, dst2 in ((wg_s, wgt_s), (wu_s, wut_s)):
        for h0 in range(0, HID, 1024):
            for di in range(8):
                ffn_steps.append(mk_ffn_step(src, dst2, h0, di, 0, pidx, False))
                pidx += 1
    for hc in range(NHC):
        ffn_steps.append(mk_ffn_step(wd_s, wdt2_s, 0, 0, hc, pidx, True))
        pidx += 1

    if stop == ("PREP", 0, 0):
        gp.wait_ge(sem_x[0].h, sem_x[0].n)
        gp.wait_ge(sem_x[1].h, sem_x[1].n)
        es.close()
        return nc

    def emit_ln(srcap, dstap, xwait=None):
        if xwait is not None:
            dve.wait_ge(xwait[0], xwait[1])
        dve.tensor_scalar(sqjunk[:, :], srcap, 1.0, 0.0, ALU.mult, ALU.add,
                          accum_out=sln[:, 0:1])
        dve.drain()
        dve.scalar_tensor_tensor(sqjunk[:, :], srcap, 1.0, srcap, ALU.mult,
                                 ALU.mult, accum_out=sln[:, 1:2])
        dve.drain()
        dve.tensor_scalar(sln[:, 2:3], sln[:, 0:1], 1.0 / D, None, ALU.mult)
        dve.tensor_scalar(sln[:, 3:4], sln[:, 1:2], 1.0 / D, None, ALU.mult)
        dve.drain()
        dve.scalar_tensor_tensor(sln[:, 4:5], sln[:, 2:3], 1.0, sln[:, 2:3],
                                 ALU.mult, ALU.mult)
        dve.drain()
        i = dve.tensor_tensor(sln[:, 5:6], sln[:, 3:4], sln[:, 4:5], ALU.subtract)
        c_dve.inc(i)
        act.wait_ge(c_dve.h, c_dve.n)
        i = act.activation(sln[:, 6:7], sln[:, 5:6], AF.Sqrt,
                           bias=cconst[:, 0:1], scale=1.0)
        c_act.inc(i)
        dve.wait_ge(c_act.h, c_act.n)
        dve.reciprocal(sln[:, 7:8], sln[:, 6:7])
        dve.drain()
        dve.tensor_tensor(sln[:, 8:9], sln[:, 2:3], sln[:, 7:8], ALU.mult)
        dve.drain()
        i = dve.tensor_scalar(dstap, srcap, sln[:, 7:8], sln[:, 8:9],
                              ALU.mult, ALU.subtract)
        return c_dve.inc(i)

    ev = {}
    carry = {
        "tanh_s01": None, "tanh_s23": None,
        "pv_slot": {0: None, 1: None},
        "smm_last": 0,
        "pv_last_final": 0, "ynorm15": 0,
        "a1_mm_last": 0, "a1_copy_last": 0,
        "a2_apply_last": 0,
        "sel_last": 0,
        "ln2_last": 0,
        "x2_res_last": 0,
        "po_free": {0: 0, 1: 0},
        "b_act_end": 0, "b_dve_end": 0,
        "omm_last": 0, "a2f6_free": 0,
    }

    def hch(di):
        if di < 4:
            return htm[:, di * 512:(di + 1) * 512]
        if di < 6:
            return sqjunk[:, (di - 4) * 512:(di - 5 + 2) * 512]
        return x2b[:, (di - 6) * 512:(di - 5) * 512]

    def emit_ln1_part1(s_i, g):
        tbase = s_i * NT_SEQ + 4 * g
        dve.wait_ge(sem_h2w.h, sem_h2w.n)
        dve.wait_ge(sem_x2w.h, sem_x2w.n)
        dve.wait_ge(sem_hw.h, sem_hw.n)
        dve.wait_ge(c_pe.h, carry["sel_last"])
        dve.wait_ge(c_pe.h, carry["a1_mm_last"])
        sxp = sem_x[(s_i * 4 + g) % 2]
        dve.wait_ge(sxp.h, sxp.n)
        lane = lambda l: sln[:, :].rearrange("p (t l) -> p t l", l=8)[:, :, l:l + 1]
        for tl in range(4):
            ti = tbase + tl
            xa = xb[:, (ti % 8) * 1024:(ti % 8 + 1) * 1024]
            dve.tensor_scalar(sqjunk[:, :], xa, 1.0, 0.0, ALU.mult, ALU.add,
                              accum_out=sln[:, 8 * tl: 8 * tl + 1])
            dve.drain()
            dve.scalar_tensor_tensor(sqjunk[:, :], xa, 1.0, xa, ALU.mult,
                                     ALU.mult, accum_out=sln[:, 8 * tl + 1: 8 * tl + 2])
            dve.drain()
        dve.tensor_scalar(lane(2), lane(0), 1.0 / D, None, ALU.mult)
        dve.tensor_scalar(lane(3), lane(1), 1.0 / D, None, ALU.mult)
        dve.drain()
        dve.scalar_tensor_tensor(lane(4), lane(2), 1.0, lane(2), ALU.mult, ALU.mult)
        dve.drain()
        i = dve.tensor_tensor(lane(5), lane(3), lane(4), ALU.subtract)
        c_dve.inc(i)
        carry["ln1_stats_done"] = c_dve.n

    def emit_ln1_part2(s_i, g, inside_b=False):
        tbase = s_i * NT_SEQ + 4 * g
        lane = lambda l: sln[:, :].rearrange("p (t l) -> p t l", l=8)[:, :, l:l + 1]
        act.wait_ge(c_dve.h, carry["ln1_stats_done"])
        if inside_b:
            act.preload_act_table(AF.Sqrt)
        i = act.activation(lane(6), lane(5), AF.Sqrt, bias=cconst[:, 0:1], scale=1.0)
        c_act.inc(i)
        if inside_b:
            act.preload_act_table(AF.Exp)
        dve.wait_ge(c_act.h, c_act.n)
        dve.reciprocal(lane(6), lane(6))
        dve.drain()
        dve.tensor_tensor(lane(7), lane(2), lane(6), ALU.mult)
        dve.drain()
        for tl in range(4):
            ti = tbase + tl
            xa = xb[:, (ti % 8) * 1024:(ti % 8 + 1) * 1024]
            if tl >= 2:
                dve.wait_ge(sem_hw.h, sem_hw.n)
            i = dve.tensor_scalar(htm[:, (tl % 2) * 1024:(tl % 2 + 1) * 1024],
                                  xa, sln[:, 8 * tl + 6: 8 * tl + 7],
                                  sln[:, 8 * tl + 7: 8 * tl + 8],
                                  ALU.mult, ALU.subtract)
            n = c_dve.inc(i)
            gp.wait_ge(c_dve.h, n)
            i = gp.dma_start(h_s[ti * 128:(ti + 1) * 128, :],
                             htm[:, (tl % 2) * 1024:(tl % 2 + 1) * 1024])
            sem_hw.inc(i)

    def emit_hT_load(s_i, g):
        tbase = s_i * NT_SEQ + 4 * g
        sp.wait_ge(sem_hw.h, sem_hw.n)
        sp.wait_ge(sem_h2w.h, sem_h2w.n)
        sp.wait_ge(sem_x2w.h, sem_x2w.n)
        sp.wait_ge(c_pe.h, carry["sel_last"])
        sp.wait_ge(c_pe.h, carry["a1_mm_last"])
        sp.wait_ge(c_dve.h, c_dve.n)
        for di in range(8):
            i = sp.dma_start(hch(di),
                             h_s[tbase * 128:tbase * 128 + 512,
                                 di * 128:(di + 1) * 128],
                             transpose=True)
            sem_hr.inc(i)

    def make_a1_units(s_i, g, standalone=False):
        tbase = s_i * NT_SEQ + 4 * g
        hT_done = sem_hr.n
        state = {"prev_copy": None, "prev2_copy": None}

        def unit(j):
            def go():
                if standalone:
                    bank = s01[:, (j % 2) * 512:(j % 2 + 1) * 512]
                else:
                    bank = f7[:, :]
                if j == 0:
                    pe.wait_ge(sem_hr.h, hT_done)
                    pe.wait_ge(c_dve.h, carry["a2_apply_last"])
                    if s_i == 0 and g == 0:
                        pe.wait_ge(c_dve.h, attn_w_done)
                elif standalone:
                    if state["prev2_copy"] is not None:
                        pe.wait_ge(c_dve.h, state["prev2_copy"])
                else:
                    pe.wait_ge(c_dve.h, state["prev_copy"])
                if j < 16:
                    which = "q" if j < 8 else "k"
                    m = j % 8
                    wT = wqT if which == "q" else wkT
                    for di in range(8):
                        i = pe.matmul(bank,
                                      wT[:, di * 1024 + m * 128: di * 1024 + (m + 1) * 128],
                                      hch(di),
                                      start=(di == 0), stop=(di == 7))
                    c_pe.inc(i)
                    dve.wait_ge(c_pe.h, c_pe.n)
                    if which == "q":
                        if j == 0:
                            dve.wait_ge(c_pe.h, carry["omm_last"])
                        dst = qyh[:, 4096 + m * 512: 4096 + (m + 1) * 512]
                    else:
                        dst = kTact[:, m * 2048 + g * 512: m * 2048 + (g + 1) * 512]
                    i = dve.tensor_copy(dst, bank)
                else:
                    tl, half = (j - 16) // 2, (j - 16) % 2
                    for di in range(8):
                        i = pe.matmul(bank,
                                      hch(di)[:, tl * 128:(tl + 1) * 128],
                                      wvT[:, di * 1024 + half * 512: di * 1024 + (half + 1) * 512],
                                      start=(di == 0), stop=(di == 7))
                    c_pe.inc(i)
                    dve.wait_ge(c_pe.h, c_pe.n)
                    tseq = 4 * g + tl
                    vt = vbuf[:, tseq * 1040:(tseq + 1) * 1040].rearrange(
                        "p (hh dd) -> p hh dd", dd=65)
                    i = dve.tensor_copy(vt[:, half * 8:(half + 1) * 8, 0:64],
                                        bank.rearrange("p (hh dd) -> p hh dd", dd=64))
                c_dve.inc(i)
                state["prev2_copy"] = state["prev_copy"]
                state["prev_copy"] = c_dve.n
                carry["a1_mm_last"] = c_pe.n
                carry["a1_copy_last"] = c_dve.n
            return go
        return [unit(j) for j in range(24)]

    def emit_A2(s_i, g):
        act.preload_act_table(AF.Sqrt)
        act.wait_ge(c_pe.h, carry["a1_mm_last"])
        act.wait_ge(c_dve.h, carry["a1_copy_last"])
        for p in range(16):
            which = "q" if p % 2 == 0 else "k"
            m = p // 2
            if which == "q":
                src = qyh[:, 4096 + m * 512: 4096 + (m + 1) * 512]
            else:
                src = kTact[:, m * 2048 + g * 512: m * 2048 + (g + 1) * 512]
            if p >= 2:
                act.wait_ge(c_pe.h, ev[("sel", which, m - 1)])
            i = act.activation(sqjunk[:, (p % 2) * 512:(p % 2 + 1) * 512],
                               src, AF.Square)
            c_act.inc(i)
            ev[("sq", which, m)] = c_act.n
            msout = s23[0:16, 0:512] if which == "q" else s23[32:48, 512:1024]
            pe.wait_ge(c_act.h, ev[("sq", which, m)])
            if p <= 1:
                tkey = carry["tanh_s23"]
                if tkey is not None:
                    pe.wait_ge(c_act.h, tkey)
            i = pe.matmul(msout,
                          csel[:, m * 16:(m + 1) * 16],
                          sqjunk[:, (p % 2) * 512:(p % 2 + 1) * 512],
                          start=(p <= 1), stop=(p >= 14))
            c_pe.inc(i)
            ev[("sel", which, m)] = c_pe.n
        carry["sel_last"] = c_pe.n
        act.wait_ge(c_pe.h, ev[("sel", "k", 7)])
        act.activation(r48f[0:16, :], s23[0:16, 0:512], AF.Sqrt,
                       bias=cconst[0:16, 1:2], scale=1.0 / 64.0)
        act.drain()
        i = act.activation(r48f[32:48, :], s23[32:48, 512:1024], AF.Sqrt,
                           bias=cconst[0:16, 1:2], scale=1.0 / 64.0)
        c_act.inc(i)
        ev["rmssqrt"] = c_act.n
        dve.wait_ge(c_act.h, c_act.n)
        with nc.allow_low_precision(reason="rms rstd recip bf16; tolerance-checked"):
            i = dve.reciprocal(rqk_b[:, :], r48f[:, :])
        c_dve.inc(i)
        rms_done = c_dve.n
        bcb = [f6, f7]
        for p in range(16):
            which = "q" if p % 2 == 0 else "k"
            m = p // 2
            if which == "q":
                rsrc, csb = rqk_b[0:16, :], cselb[0:16, :]
            else:
                rsrc, csb = rqk_b[32:48, :], cselb[32:48, :]
            bank = bcb[p % 2]
            if p < 2:
                pe.wait_ge(c_dve.h, rms_done)
                if p == 0:
                    pe.wait_ge(c_dve.h, carry["ynorm15"])
                else:
                    pe.wait_ge(c_dve.h, carry["a1_copy_last"])
            else:
                pe.wait_ge(c_dve.h, ev[("app", which, m - 1)])
            i = pe.matmul(bank[:, :], csb[:, m * 128:(m + 1) * 128],
                          rsrc, start=True, stop=True)
            c_pe.inc(i)
            dve.wait_ge(c_pe.h, c_pe.n)
            if which == "q":
                if p == 0:
                    dve.wait_ge(c_pe.h, carry["smm_last"])
                dst = hTq[:, m * 512:(m + 1) * 512]
                src0 = qyh[:, 4096 + m * 512: 4096 + (m + 1) * 512]
                i = dve.scalar_tensor_tensor(dst, src0, 0.125, bank[:, :],
                                             ALU.mult, ALU.mult)
            else:
                win = kTact[:, m * 2048 + g * 512: m * 2048 + (g + 1) * 512]
                i = dve.scalar_tensor_tensor(win, win, 1.0, bank[:, :],
                                             ALU.mult, ALU.mult)
            c_dve.inc(i)
            ev[("app", which, m)] = c_dve.n
        carry["a2_apply_last"] = c_dve.n

    def emit_C(s_i, g):
        tbase = s_i * NT_SEQ + 4 * g
        act.preload_act_table(AF.Sqrt)
        dve.wait_ge(sem_x2w.h, sem_x2w.n)
        dve.wait_ge(sem_hw.h, sem_hw.n)
        dve.wait_ge(sem_h2w.h, sem_h2w.n)
        dve.wait_ge(c_pe.h, carry["a1_mm_last"])
        for tl in range(4):
            ti = tbase + tl
            for half in range(2):
                bank = s01[:, half * 512:(half + 1) * 512]
                if tl >= 1:
                    pe.wait_ge(c_dve.h, ev[("x2", tl - 1, half)])
                else:
                    pe.wait_ge(c_dve.h, carry["ynorm15"])
                    pe.wait_ge(c_act.h, carry["tanh_s01"])
                for di in range(8):
                    i = pe.matmul(
                        bank,
                        qyh[:, di * 512 + tl * 128: di * 512 + (tl + 1) * 128],
                        woT[:, di * 1024 + half * 512: di * 1024 + (half + 1) * 512],
                        start=(di == 0), stop=(di == 7))
                c_pe.inc(i)
                ev[("omm", tl, half)] = c_pe.n
            if tl >= 1:
                dve.wait_ge(sem_x2w.h, sem_x2w.n)
            dve.drain()
            x2slot = [x2b[:, :], htm[:, 0:1024], htm[:, 1024:2048], sqjunk[:, :]][tl]
            for half in range(2):
                bank = s01[:, half * 512:(half + 1) * 512]
                dve.wait_ge(c_pe.h, ev[("omm", tl, half)])
                i = dve.tensor_tensor(
                    x2slot[:, half * 512:(half + 1) * 512], bank,
                    xb[:, (ti % 8) * 1024 + half * 512:(ti % 8) * 1024 + (half + 1) * 512],
                    ALU.add)
                c_dve.inc(i)
                ev[("x2", tl, half)] = c_dve.n
            gp.wait_ge(c_dve.h, ev[("x2", tl, 1)])
            i = gp.dma_start(x2_s[ti * 128:(ti + 1) * 128, :], x2slot)
            sem_x2w.inc(i)
        slots = [x2b[:, :], htm[:, 0:1024], htm[:, 1024:2048], sqjunk[:, :]]
        lane = lambda l: sln[:, :].rearrange("p (t l) -> p t l", l=8)[:, :, l:l + 1]
        dve.drain()
        for tl in range(4):
            dve.tensor_scalar(pTb[:, 0:1024], slots[tl], 1.0, 0.0, ALU.mult,
                              ALU.add, accum_out=sln[:, 8 * tl: 8 * tl + 1])
            dve.drain()
            dve.scalar_tensor_tensor(pTb[:, 0:1024], slots[tl], 1.0, slots[tl],
                                     ALU.mult, ALU.mult,
                                     accum_out=sln[:, 8 * tl + 1: 8 * tl + 2])
            dve.drain()
        dve.tensor_scalar(lane(2), lane(0), 1.0 / D, None, ALU.mult)
        dve.tensor_scalar(lane(3), lane(1), 1.0 / D, None, ALU.mult)
        dve.drain()
        dve.scalar_tensor_tensor(lane(4), lane(2), 1.0, lane(2), ALU.mult, ALU.mult)
        dve.drain()
        i = dve.tensor_tensor(lane(5), lane(3), lane(4), ALU.subtract)
        c_dve.inc(i)
        act.wait_ge(c_dve.h, c_dve.n)
        i = act.activation(lane(6), lane(5), AF.Sqrt, bias=cconst[:, 0:1], scale=1.0)
        c_act.inc(i)
        dve.wait_ge(c_act.h, c_act.n)
        dve.reciprocal(lane(6), lane(6))
        dve.drain()
        dve.tensor_tensor(lane(7), lane(2), lane(6), ALU.mult)
        dve.drain()
        dve.wait_ge(sem_x2w.h, sem_x2w.n)
        for tl in range(4):
            ti = tbase + tl
            i = dve.tensor_scalar(slots[tl], slots[tl],
                                  sln[:, 8 * tl + 6: 8 * tl + 7],
                                  sln[:, 8 * tl + 7: 8 * tl + 8],
                                  ALU.mult, ALU.subtract)
            n = c_dve.inc(i)
            gp.wait_ge(c_dve.h, n)
            i = gp.dma_start(h2_s[ti * 128:(ti + 1) * 128, :], slots[tl])
            sem_h2w.inc(i)
        carry["omm_last"] = ev[("omm", 3, 1)]
        carry["po_free"] = {0: ev[("x2", 3, 0)], 1: ev[("x2", 3, 1)]}
        carry["ln2_last"] = c_dve.n
        carry["x2_res_last"] = ev[("x2", 3, 1)]
        if g + 2 < NG:
            gp.wait_ge(c_dve.h, ev[("x2", 3, 1)])
            for tl in range(4):
                ti = tbase + 8 + tl
                if ti not in x_done:
                    load_x(ti)

    def emit_B(s_i, g, next_g, extra=()):
        tbase = s_i * NT_SEQ + 4 * g
        a1_units = []
        act.preload_act_table(AF.Exp)
        items = [("pair", kt0) for kt0 in range(0, 4 * g + 4, 2)]
        n_items = len(items)
        n_b = H * n_items
        n_units = (24 if next_g is not None else 0) + len(extra)
        inject_every = max(1, (n_b - 8) // (n_units + 1)) if n_units else 10 ** 9
        spair = [s01, s23]
        tanh_keys = ["tanh_s01", "tanh_s23"]
        last_pv_on_slot = carry["pv_slot"]

        def emit_normA(h_i):
            pyb = pyv[h_i % 2]
            dve.wait_ge(c_pe.h, ev[("pv_last", h_i)])
            with nc.allow_low_precision(reason="softmax denom recip bf16; tolerance-checked"):
                i = dve.reciprocal(recb, pyb[64:65, :])
            c_dve.inc(i)
            ev[("recb", h_i)] = c_dve.n

        def emit_normB(h_i):
            chunk, part = h_i // 2, (h_i % 2) * 64
            pyb = pyv[h_i % 2]
            pe.wait_ge(c_dve.h, ev[("recb", h_i)])
            if h_i >= 1:
                pe.wait_ge(c_dve.h, ev[("ynorm", h_i - 1)])
            else:
                pe.wait_ge(c_dve.h, carry["a2f6_free"])
            i = pe.matmul(f6[0:64, :], cones[:, :], recb,
                          start=True, stop=True)
            c_pe.inc(i)
            ev[("ybc", h_i)] = c_pe.n
            dve.wait_ge(c_pe.h, ev[("ybc", h_i)])
            dve.tensor_copy(rb[:, :], f6[0:64, :])
            dve.drain()
            i = dve.tensor_tensor(
                qyh[part:part + 64, chunk * 512:(chunk + 1) * 512],
                pyb[0:64, :], rb[:, :], ALU.mult)
            c_dve.inc(i)
            ev[("ynorm", h_i)] = c_dve.n

        def emit_S(h_i, it_i):
            chunk, part = h_i // 2, (h_i % 2) * 64
            kind, kt0 = items[it_i]
            sp_i = (h_i * n_items + it_i) % 2
            sb = spair[sp_i]
            if carry[tanh_keys[sp_i]] is not None:
                pe.wait_ge(c_act.h, carry[tanh_keys[sp_i]])
            if h_i == 0 and it_i < 2:
                pe.wait_ge(c_dve.h, carry["a2_apply_last"])
                if sp_i == 0:
                    pe.wait_ge(c_dve.h, carry["ln2_last"])
                else:
                    pe.wait_ge(c_pe.h, carry["sel_last"])
            for u in range(2):
                kt = kt0 + u
                i = pe.matmul(
                    sb[0:128, u * 512:(u + 1) * 512],
                    kTact[part:part + 64,
                          chunk * 2048 + kt * 128: chunk * 2048 + (kt + 1) * 128],
                    hTq[part:part + 64,
                        chunk * 512:(chunk + 1) * 512],
                    start=True, stop=True)
            c_pe.inc(i)
            ev[("smm", h_i, it_i)] = c_pe.n
            act.wait_ge(c_pe.h, ev[("smm", h_i, it_i)])
            if last_pv_on_slot[sp_i] is not None:
                act.wait_ge(c_pe.h, last_pv_on_slot[sp_i])
            off0 = (kt0 - 4 * g) * 128 if kt0 >= 4 * g else 0
            i = act.activation(tanhsil[:, sp_i * 1024 + off0: (sp_i + 1) * 1024],
                               sb[0:128, off0:1024],
                               AF.Tanh, scale=1.0 / SOFTCAP)
            c_act.inc(i)
            carry[tanh_keys[sp_i]] = c_act.n
            act.drain()
            i = act.activation(pTb[:, sp_i * 1024 + off0: (sp_i + 1) * 1024],
                               tanhsil[:, sp_i * 1024 + off0: (sp_i + 1) * 1024],
                               AF.Exp, scale=SOFTCAP)
            c_act.inc(i)
            ev[("exp", h_i, it_i)] = c_act.n
            for u in range(2):
                kt = kt0 + u
                if kt >= 4 * g:
                    j = kt - 4 * g
                    off = sp_i * 1024 + u * 512 + j * 128
                    gp.wait_ge(c_act.h, ev[("exp", h_i, it_i)])
                    i = gp.tensor_tensor(pTb[:, off: off + 128],
                                         pTb[:, off: off + 128],
                                         cut[:, :], ALU.mult)
                    c_gp.inc(i)
                    ev[("mask", h_i, it_i, u)] = c_gp.n

        def emit_PV(h_i, it_i):
            pyb = pyv[h_i % 2]
            kind, kt0 = items[it_i]
            sp_i = (h_i * n_items + it_i) % 2
            if it_i == 0:
                if h_i >= 2:
                    pe.wait_ge(c_dve.h, ev[("ynorm", h_i - 2)])
                else:
                    pe.wait_ge(c_dve.h, carry["b_dve_end"])
            for u in range(2):
                kt = kt0 + u
                j = kt - 4 * g
                qoff = j * 128 if j > 0 else 0
                if kt >= 4 * g:
                    pe.wait_ge(c_gp.h, ev[("mask", h_i, it_i, u)])
                elif u == 0:
                    pe.wait_ge(c_act.h, ev[("exp", h_i, it_i)])
                i = pe.matmul(
                    pyb[0:65, qoff:512],
                    vbuf[:, kt * 1040 + h_i * 65: kt * 1040 + (h_i + 1) * 65],
                    pTb[:, sp_i * 1024 + u * 512 + qoff: sp_i * 1024 + (u + 1) * 512],
                    start=(it_i == 0 and u == 0),
                    stop=(it_i == n_items - 1 and u == 1))
                c_pe.inc(i)
            last_pv_on_slot[sp_i] = c_pe.n
            if it_i == n_items - 1:
                ev[("pv_last", h_i)] = c_pe.n

        stream = [(h_i, it_i) for h_i in range(H) for it_i in range(n_items)]
        a1_i = 0
        for idx, (h_i, it_i) in enumerate(stream):
            emit_S(h_i, it_i)
            if idx >= 1:
                emit_PV(*stream[idx - 1])
            if it_i == 0 and h_i >= 1:
                emit_normA(h_i - 1)
            if it_i == min(2, n_items - 1) and h_i >= 1:
                emit_normB(h_i - 1)
            if n_units:
                if idx == 2 and next_g is not None:
                    emit_ln1_part2(s_i, next_g, inside_b=True)
                elif idx == 5:
                    if next_g is not None:
                        emit_hT_load(s_i, next_g)
                        a1_units = make_a1_units(s_i, next_g) + list(extra)
                    else:
                        a1_units = list(extra)
                elif idx >= 8:
                    while a1_i < len(a1_units) and \
                            a1_i < (idx - 7) // inject_every + 1 and \
                            (idx - 7) % inject_every == 0 and a1_i <= (idx - 7) // inject_every:
                        a1_units[a1_i]()
                        a1_i += 1
        emit_PV(*stream[-1])
        while a1_i < len(a1_units):
            a1_units[a1_i]()
            a1_i += 1
        emit_normA(H - 1)
        emit_normB(H - 1)
        carry["smm_last"] = ev[("smm", H - 1, n_items - 1)]
        carry["pv_last_final"] = ev[("pv_last", H - 1)]
        carry["ynorm15"] = ev[("ynorm", H - 1)]
        carry["b_act_end"] = c_act.n
        carry["b_dve_end"] = c_dve.n

    pre_done = set()
    for s_i in range(SEQ_PER_CORE):
        act.preload_act_table(AF.Sqrt)
        vv = vbuf[:, :].rearrange("p (gg dd) -> p gg dd", dd=65)
        c_dve.inc(dve.memset(vv[:, :, 64:65], 1.0))
        for tl in range(4):
            ti = s_i * NT_SEQ + 4 + tl
            if ti not in x_done:
                load_x(ti)
        carry["a2f6_free"] = 0
        if s_i not in pre_done:
            emit_ln1_part1(s_i, 0)
            emit_ln1_part2(s_i, 0)
            emit_hT_load(s_i, 0)
        for u in make_a1_units(s_i, 0, standalone=True):
            u()
        emit_A2(s_i, 0)
        carry["a2f6_free"] = carry["a2_apply_last"]
        for g in range(NG):
            if g > 0:
                emit_A2(s_i, g)
                carry["a2f6_free"] = carry["a2_apply_last"]
                emit_C(s_i, g - 1)
            if g + 1 < NG:
                emit_ln1_part1(s_i, g + 1)
            if s_i == 0:
                lo, hi = FFN_SPLIT[g], FFN_SPLIT[g + 1]
                extra = ffn_steps[lo:hi]
            else:
                extra = ()
            emit_B(s_i, g, g + 1 if g + 1 < NG else None, extra=extra)
        emit_C(s_i, NG - 1)
        if s_i + 1 < SEQ_PER_CORE:
            gp.wait_ge(c_dve.h, carry["x2_res_last"])
            for tl in range(4):
                ti = (s_i + 1) * NT_SEQ + tl
                if ti not in x_done:
                    load_x(ti)
            emit_ln1_part1(s_i + 1, 0)
            emit_ln1_part2(s_i + 1, 0)
            emit_hT_load(s_i + 1, 0)
            pre_done.add(s_i + 1)

        nc.all_engine_barrier()
        act.preload_act_table(AF.Silu)
        sp.wait_ge(sem_h2w.h, sem_h2w.n)
        gp.wait_ge(sem_x2w.h, sem_x2w.n)
        sp.wait_ge(sem_fpp[0].h, sem_fpp[0].n)
        sp.wait_ge(sem_fpp[1].h, sem_fpp[1].n)
        if s_i + 1 < SEQ_PER_CORE:
            for tl in range(4):
                if (s_i + 1) * NT_SEQ + tl not in x_done:
                    load_x((s_i + 1) * NT_SEQ + tl)
        last_out_dma = {0: None, 1: None}

        def emit_h2r(tc, dis=range(8)):
            hslot = (tc % 2) * 4096
            for di in dis:
                i = sp.dma_start(
                    qyh[:, hslot + di * 512: hslot + (di + 1) * 512],
                    h2_s[s_i * S + tc * 512: s_i * S + (tc + 1) * 512,
                         di * 128:(di + 1) * 128],
                    transpose=True)
                sem_h2r[tc % 2].inc(i)
            ev[("h2r", tc)] = sem_h2r[tc % 2].n

        for tc in range(4):
            slot = (tc % 2) * 4096
            if tc >= 2:
                sp.wait_ge(c_pe.h, ev[("guD", tc - 2)])
            emit_h2r(tc)
            for tl in range(4):
                ti_seq = tc * 4 + tl
                if tc >= 1:
                    gp.wait_ge(c_dve.h, ev[("outD", tc - 1, 1, tl)])
                i = gp.dma_start(
                    xb[:, (4 + tl) * 1024:(5 + tl) * 1024],
                    x2_s[s_i * S + ti_seq * 128: s_i * S + (ti_seq + 1) * 128, :])
                sem_x2r.inc(i)
            ev[("x2r", tc)] = sem_x2r.n

            if tc > 0 or s_i > 0:
                sp.wait_ge(c_pe.h, ev.get(("dmm_prev",), 0))
            i = sp.dma_start(
                vbuf[:, 0:12288],
                wdt2_s[0].rearrange("p c d -> p (c d)"))
            sem_fwd.inc(i)
            for hc in range(NHC):
                wslot = (hc % 2) * 1024
                if hc >= 2:
                    sp.wait_ge(c_pe.h, ev[("gu", tc, hc - 2)])
                sfw = sem_fw[hc % 2]
                sfw.inc(sp.dma_start(wgub[:, wslot:wslot + 1024], wgt_s[hc, :, :]))
                sfw.inc(sp.dma_start(wuub[:, wslot:wslot + 1024], wut_s[hc, :, :]))
                ev[("fw", tc, hc)] = sfw.n
                pe.wait_ge(sfw.h, ev[("fw", tc, hc)])
                if hc == 0:
                    pe.wait_ge(sem_h2r[tc % 2].h, ev[("h2r", tc)])
                    if s_i == 0 and tc == 0:
                        pe.wait_ge(sem_fpp[0].h, sem_fpp[0].n)
                        pe.wait_ge(sem_fpp[1].h, sem_fpp[1].n)
                gbank = pqv[0] if hc % 2 == 0 else f6[:, :]
                ubank = pqv[1] if hc % 2 == 0 else f7[:, :]
                if hc >= 2:
                    pe.wait_ge(c_act.h, ev[("silu", tc, hc - 2)])
                for di in range(8):
                    i = pe.matmul(gbank,
                                  wgub[:, wslot + di * 128: wslot + (di + 1) * 128],
                                  qyh[:, slot + di * 512: slot + (di + 1) * 512],
                                  start=(di == 0), stop=(di == 7))
                c_pe.inc(i)
                ev[("gmm", tc, hc)] = c_pe.n
                if hc >= 2:
                    pe.wait_ge(c_dve.h, ev[("actT", tc, hc - 2)])
                for di in range(8):
                    i = pe.matmul(ubank,
                                  wuub[:, wslot + di * 128: wslot + (di + 1) * 128],
                                  qyh[:, slot + di * 512: slot + (di + 1) * 512],
                                  start=(di == 0), stop=(di == 7))
                c_pe.inc(i)
                ev[("gu", tc, hc)] = c_pe.n
                act.wait_ge(c_pe.h, ev[("gmm", tc, hc)])
                if hc >= 2:
                    act.wait_ge(c_dve.h, ev[("actT", tc, hc - 2)])
                i = act.activation(tanhsil[:, (hc % 2) * 512:(hc % 2 + 1) * 512],
                                   gbank, AF.Silu, scale=DEQ)
                c_act.inc(i)
                ev[("silu", tc, hc)] = c_act.n
                dve.wait_ge(c_act.h, ev[("silu", tc, hc)])
                dve.wait_ge(c_pe.h, ev[("gu", tc, hc)])
                if tc >= 1 and hc == 0:
                    dve.wait_ge(c_pe.h, ev[("dmm", tc - 1, 1, 3)])
                i = dve.scalar_tensor_tensor(
                    kTact[:, hc * 512:(hc + 1) * 512], ubank, DEQ,
                    tanhsil[:, (hc % 2) * 512:(hc % 2 + 1) * 512],
                    ALU.mult, ALU.mult)
                c_dve.inc(i)
                ev[("actT", tc, hc)] = c_dve.n

            ev[("guD", tc)] = c_pe.n

            for half in range(2):
                if half == 1:
                    sp.wait_ge(c_pe.h, ev[("dmm", tc, 0, 3)])
                    i = sp.dma_start(
                        vbuf[:, 0:12288],
                        wdt2_s[1].rearrange("p c d -> p (c d)"))
                    sem_fwd.inc(i)
                pe.wait_ge(sem_fwd.h, sem_fwd.n)
                pe.wait_ge(c_dve.h, ev[("actT", tc, NHC - 1)])
                for tl in range(4):
                    bank = pdv[tl]
                    if half == 1:
                        pe.wait_ge(c_dve.h, ev[("outD", tc, 0, tl)])
                    elif tc >= 1:
                        pe.wait_ge(c_dve.h, ev[("outD", tc - 1, 1, tl)])
                    for hc in range(NHC):
                        i = pe.matmul(bank[:, :],
                                      kTact[:, hc * 512 + tl * 128: hc * 512 + (tl + 1) * 128],
                                      vbuf[:, hc * 512:(hc + 1) * 512],
                                      start=(hc == 0), stop=(hc == NHC - 1))
                    c_pe.inc(i)
                    ev[("dmm", tc, half, tl)] = c_pe.n
                    ev[("dmm_prev",)] = c_pe.n
                for tl in range(4):
                    bank = pdv[tl]
                    dve.wait_ge(c_pe.h, ev[("dmm", tc, half, tl)])
                    dve.wait_ge(sem_x2r.h, ev[("x2r", tc)])
                    if sem_out.n > 0:
                        dve.wait_ge(sem_out.h, sem_out.n)
                    i = dve.scalar_tensor_tensor(
                        outb[:, :], bank[:, :], DEQ,
                        xb[:, (4 + tl) * 1024 + half * 512:(4 + tl) * 1024 + (half + 1) * 512],
                        ALU.mult, ALU.add)
                    c_dve.inc(i)
                    ev[("outD", tc, half, tl)] = c_dve.n
                    gp.wait_ge(c_dve.h, ev[("outD", tc, half, tl)])
                    ti = s_i * NT_SEQ + tc * 4 + tl
                    i = gp.dma_start(
                        out_h[ti * 128:(ti + 1) * 128, half * 512:(half + 1) * 512],
                        outb[:, :])
                    sem_out.inc(i)
                    last_out_dma[half] = sem_out.n

        nc.all_engine_barrier()

    gp.wait_ge(sem_out.h, sem_out.n)
    es.close()
    return nc


def _make_consts():
    cut = np.triu(np.ones((128, 128), np.float32)).astype(BF16)
    csel = np.zeros((128, 8, 16), np.float32)
    for m in range(8):
        for p in range(128):
            csel[p, m, 2 * m + p // 64] = 1.0
    csel = csel.reshape(128, 128).astype(BF16)
    cselb = np.zeros((16, 8, 128), np.float32)
    for m in range(8):
        for p in range(128):
            cselb[2 * m + p // 64, m, p] = 1.0
    cselb = cselb.reshape(16, 1024).astype(BF16)
    cones = np.ones((1, 64), np.float32).astype(BF16)
    cconst = np.zeros((128, 4), np.float32)
    cconst[:, 0] = EPS_LN
    cconst[:, 1] = EPS_RMS
    cid = np.eye(128, dtype=np.float32).astype(BF16)
    return cut, csel, cselb, cones, cconst, cid


_NC_CACHE = {}


def kernel(L0_attn_x, L0_attn_w_q, L0_attn_w_k, L0_attn_w_v, L0_attn_w_o,
           L0_ffn_w_gate, L0_ffn_w_up, L0_ffn_w_down):
    x = np.ascontiguousarray(np.asarray(L0_attn_x, dtype=np.float32)).reshape(B, S, D)
    wq = np.ascontiguousarray(np.asarray(L0_attn_w_q, dtype=np.float32))
    wk = np.ascontiguousarray(np.asarray(L0_attn_w_k, dtype=np.float32))
    wv = np.ascontiguousarray(np.asarray(L0_attn_w_v, dtype=np.float32))
    wo = np.ascontiguousarray(np.asarray(L0_attn_w_o, dtype=np.float32))
    wg = np.ascontiguousarray(np.asarray(L0_ffn_w_gate, dtype=np.int32))
    wu = np.ascontiguousarray(np.asarray(L0_ffn_w_up, dtype=np.int32))
    wd = np.ascontiguousarray(np.asarray(L0_ffn_w_down, dtype=np.int32))
    cut, csel, cselb, cones, cconst, cid = _make_consts()

    if "nc" not in _NC_CACHE:
        _NC_CACHE["nc"] = build_nc()
    nc = _NC_CACHE["nc"]

    in_maps = []
    for c in range(NC):
        shard = np.ascontiguousarray(
            x[c * SEQ_PER_CORE:(c + 1) * SEQ_PER_CORE].reshape(TOK, D))
        in_maps.append({
            "x": shard, "wq": wq, "wk": wk, "wv": wv, "wo": wo,
            "wg": wg, "wu": wu, "wd": wd,
            "cut": cut, "csel": csel, "cselb": cselb,
            "cones": cones, "cconst": cconst, "cid": cid,
        })
    res = run_bass_kernel_spmd(nc, in_maps, core_ids=list(range(NC)))
    outs = [np.asarray(res.results[c]["out"], dtype=np.float32).reshape(
        SEQ_PER_CORE, S, D) for c in range(NC)]
    return np.concatenate(outs, axis=0)
